# version 11
# baseline (speedup 1.0000x reference)
import numpy as np
import ml_dtypes
import concourse.bass as bass
import concourse.mybir as mybir
from concourse.alu_op_type import AluOpType
from concourse.bass_utils import run_bass_kernel_spmd
from contextlib import ExitStack

F32 = mybir.dt.float32
BF16 = mybir.dt.bfloat16
AF = mybir.ActivationFunctionType
OP = AluOpType

NCORES = 8
T = 2048
D = 2048
DC = 16
RC = 16
HCN = 4
I_DIM = 8192
ICN = 64
NL = 16
XST = 2052
RSQ_D = float(1.0 / np.sqrt(2048.0))
RSQ_I = float(1.0 / np.sqrt(8192.0))
ENGS = ["sync", "tensor", "vector", "scalar", "gpsimd"]

DEBUG = False
DEBUG_TAPS = None


class Prog:

    NDMA = 6

    def __init__(self, ctx, nc):
        self.nc = nc
        self.items = {g: [] for g in ENGS}
        self.sems = {}
        self.cnt = {}
        names = ["pe", "dv", "ac", "pl", "cc"]
        self.pool = {"sync": [], "gpsimd": []}
        for q, pfx in [("sync", "dm"), ("gpsimd", "dg")]:
            for i in range(self.NDMA):
                names.append(f"{pfx}{i}")
                self.pool[q].append(f"{pfx}{i}")
        for name in names:
            self.sems[name] = ctx.enter_context(nc.semaphore("s_" + name))
            self.cnt[name] = 0
        self.qidx = {"sync": 0, "gpsimd": 0}
        self.cur_phase = "init"

    def emit(self, eng, fn, wait=(), inc=None, inck=1):
        norm = []
        for entry in wait:
            sn, v = entry
            if isinstance(v, tuple):
                sn, v = v
            norm.append((sn, v))
        wait = [(sn, v) for (sn, v) in norm if v > 0]
        if inc is not None:
            self.cnt[inc] += inck
        newv = self.cnt[inc] if inc is not None else None
        sems = self.sems

        def item(engobj, fn=fn, wait=wait, inc=inc, inck=inck):
            for sn, v in wait:
                engobj.wait_ge(sems[sn], v)
            instr = fn(engobj)
            if inc is not None:
                instr.then_inc(sems[inc], inck)
        self.items[eng].append((item, wait, inc, inck, self.cur_phase))
        return newv

    def dma(self, eng, out, in_, wait=()):
        q = eng if eng in self.pool else "gpsimd"
        sem = self.pool[q][self.qidx[q] % self.NDMA]
        self.qidx[q] += 1
        wait = [(sem, self.cnt[sem])] + list(wait)
        v = self.emit(eng, lambda e, out=out, in_=in_: e.dma_start(out=out, in_=in_),
                      wait=wait, inc=sem, inck=16)
        return (sem, v)

    def queue_barrier(self, q):
        return [(s, self.cnt[s]) for s in self.pool[q] if self.cnt[s] > 0]

    def n(self, name):
        return self.cnt[name]

    def check(self):
        vals = {k: 0 for k in self.sems}
        pos = {g: 0 for g in ENGS}
        progressed = True
        while progressed:
            progressed = False
            for g in ENGS:
                while pos[g] < len(self.items[g]):
                    _, wait, inc, inck, _ph = self.items[g][pos[g]]
                    if all(vals[sn] >= v for sn, v in wait):
                        if inc is not None:
                            vals[inc] += inck
                        pos[g] += 1
                        progressed = True
                    else:
                        break
        stuck = {g: (pos[g], len(self.items[g])) for g in ENGS if pos[g] < len(self.items[g])}
        if stuck:
            msgs = []
            for g, (p, n) in stuck.items():
                _, wait, _, _, _ph = self.items[g][p]
                msgs.append(f"{g}@{p}/{n} waits {wait} have "
                            f"{[(sn, vals[sn]) for sn, _ in wait]}")
            raise RuntimeError("build-time deadlock: " + "; ".join(msgs))

    def set_phase(self, name):
        self.cur_phase = name

    def finalize(self, block):
        nc = self.nc
        for g in ENGS:
            items = self.items[g]

            def body(engobj, items=items):
                cur = None
                sid = None
                for it in items:
                    ph = it[4]
                    if ph != cur:
                        if cur is not None:
                            nc.leave_named_scope(cur, sid, False)
                        sid, _ = nc.enter_named_scope(ph, False)
                        cur = ph
                    it[0](engobj)
                if cur is not None:
                    nc.leave_named_scope(cur, sid, False)
            getattr(block, g)(body)


def build_program():
    nc = bass.Bass("TRN2", target_bir_lowering=False)

    xs = nc.dram_tensor("xs", [T + 2, D], F32, kind="ExternalInput").ap()
    wu_u = nc.dram_tensor("wu_u", [RC, DC, 128, 128], F32, kind="ExternalInput").ap()
    wg1 = nc.dram_tensor("wg1", [HCN, DC, 128, 128], F32, kind="ExternalInput").ap()
    wg2 = nc.dram_tensor("wg2", [RC, HCN, 128, 128], F32, kind="ExternalInput").ap()
    wv_sl = nc.dram_tensor("wv_sl", [2, DC, 128, 128], F32, kind="ExternalInput").ap()
    wuv_sl = nc.dram_tensor("wuv_sl", [8, 128, 512], F32, kind="ExternalInput").ap()
    wup_sl = nc.dram_tensor("wup_sl", [4, DC, 128, 256], F32, kind="ExternalInput").ap()
    wgt_sl = nc.dram_tensor("wgt_sl", [4, DC, 128, 256], F32, kind="ExternalInput").ap()
    wdn_sl = nc.dram_tensor("wdn_sl", [2, 128, I_DIM], F32, kind="ExternalInput").ap()
    bu_i = nc.dram_tensor("bu", [128, RC], F32, kind="ExternalInput").ap()
    bv_i = nc.dram_tensor("bv", [128, RC], F32, kind="ExternalInput").ap()
    bg1_i = nc.dram_tensor("bg1", [128, HCN], F32, kind="ExternalInput").ap()
    bg2_i = nc.dram_tensor("bg2", [128, RC], F32, kind="ExternalInput").ap()
    cw_i = nc.dram_tensor("cw", [128, DC * 3], F32, kind="ExternalInput").ap()
    cb_i = nc.dram_tensor("cb", [128, DC], F32, kind="ExternalInput").ap()
    su_i = nc.dram_tensor("su", [128, DC], F32, kind="ExternalInput").ap()
    sud_i = nc.dram_tensor("sud", [128, ICN], F32, kind="ExternalInput").ap()
    wsu_i = nc.dram_tensor("wsu", [128, ICN], F32, kind="ExternalInput").ap()
    wsg_i = nc.dram_tensor("wsg", [128, ICN], F32, kind="ExternalInput").ap()
    wsd_i = nc.dram_tensor("wsd", [128, 2], F32, kind="ExternalInput").ap()
    mask_i = nc.dram_tensor("maskt", [128, NCORES], F32, kind="ExternalInput").ap()
    id_i = nc.dram_tensor("ident", [128, 128], BF16, kind="ExternalInput").ap()
    h128_i = nc.dram_tensor("h128", [128, 128], BF16, kind="ExternalInput").ap()

    out = nc.dram_tensor("out", [T, D], F32, kind="ExternalOutput").ap()

    ag_wv_in = nc.dram_tensor("ag_wv_in", [2, DC, 128, 128], BF16)
    ag_wv = nc.dram_tensor("ag_wv", [RC, DC, 128, 128], BF16)
    ag_wuv_in = nc.dram_tensor("ag_wuv_in", [8, 128, 512], BF16)
    ag_wuv = nc.dram_tensor("ag_wuv", [64, 128, 512], BF16)
    ag_wup_in = nc.dram_tensor("ag_wup_in", [4, DC, 128, 256], BF16)
    ag_wup = nc.dram_tensor("ag_wup", [32, DC, 128, 256], BF16)
    ag_wgt_in = nc.dram_tensor("ag_wgt_in", [4, DC, 128, 256], BF16)
    ag_wgt = nc.dram_tensor("ag_wgt", [32, DC, 128, 256], BF16)
    ag_wdn_in = nc.dram_tensor("ag_wdn_in", [ICN, 128, 256], BF16)
    ag_wdn = nc.dram_tensor("ag_wdn", [NCORES, ICN, 128, 256], BF16)
    ag_s_in = nc.dram_tensor("ag_s_in", [1, T], F32)
    ag_s = nc.dram_tensor("ag_s", [NCORES, T], F32)
    x1d = nc.dram_tensor("x1d", [T, D], F32)
    wdn_b = nc.dram_tensor("wdn_b", [2, 128, I_DIM], F32)

    taps = (set() if not DEBUG else
            ({"xn", "hb", "kc", "soff", "go", "x1", "wdn", "xb", "acc", "y0", "gs0"}
             if DEBUG_TAPS is None else set(DEBUG_TAPS)))
    dbg = {}
    if True:
        for name, shape, dt in [
            ("xn", [128, DC * XST], BF16), ("hb", [128, HCN * T], BF16),
            ("kc", [128, RC * T], BF16), ("go", [128, RC * T], BF16),
            ("soff", [128, NCORES * RC + RC], F32), ("x1", [T, D], F32),
            ("wdn", [ICN, 128, 256], BF16), ("xb", [128, DC * 1024], BF16),
            ("acc", [128, 8 * 2048], F32), ("y0", [128, 1024], BF16),
        ]:
            if name in taps:
                dbg[name] = nc.dram_tensor("dbg_" + name, shape, dt, kind="ExternalOutput").ap()

    octx = ExitStack()
    with octx:
        e = octx.enter_context
        P = Prog(octx, nc)

        c_bu = e(nc.sbuf_tensor("c_bu", [128, RC], F32))
        c_bv = e(nc.sbuf_tensor("c_bv", [128, RC], F32))
        c_bg1 = e(nc.sbuf_tensor("c_bg1", [128, HCN], F32))
        c_bg2 = e(nc.sbuf_tensor("c_bg2", [128, RC], F32))
        c_cw = e(nc.sbuf_tensor("c_cw", [128, DC * 3], F32))
        c_cb = e(nc.sbuf_tensor("c_cb", [128, DC], F32))
        c_su = e(nc.sbuf_tensor("c_su", [128, DC], F32))
        c_sud = e(nc.sbuf_tensor("c_sud", [128, ICN], F32))
        c_wsu = e(nc.sbuf_tensor("c_wsu", [128, ICN], F32))
        c_wsg = e(nc.sbuf_tensor("c_wsg", [128, ICN], F32))
        c_wsd = e(nc.sbuf_tensor("c_wsd", [128, 2], F32))
        c_mask = e(nc.sbuf_tensor("c_mask", [128, NCORES], F32))
        c_id = e(nc.sbuf_tensor("c_id", [128, 128], BF16))
        c_h128 = e(nc.sbuf_tensor("c_h128", [128, 128], BF16))
        c_hsu = e(nc.sbuf_tensor("c_hsu", [128, DC * 128], BF16))
        c_ones = e(nc.sbuf_tensor("c_ones", [128, T], BF16))
        st_s1 = e(nc.sbuf_tensor("st_s1", [128, NL], F32))
        st_s2 = e(nc.sbuf_tensor("st_s2", [128, NL], F32))
        st_m = e(nc.sbuf_tensor("st_m", [128, NL], F32))
        st_r = e(nc.sbuf_tensor("st_r", [128, NL], F32))
        st_t = e(nc.sbuf_tensor("st_t", [128, NL], F32))
        st_n = e(nc.sbuf_tensor("st_n", [128, NL], F32))
        c_eps = e(nc.sbuf_tensor("c_eps", [128, 1], F32))
        sfm = e(nc.sbuf_tensor("sfm", [128, RC], F32))
        sprt = e(nc.sbuf_tensor("sprt", [128, 4], F32))
        sall = e(nc.sbuf_tensor("sall", [128, NCORES * RC], F32))
        soff = e(nc.sbuf_tensor("soff", [128, RC], F32))
        GB = e(nc.sbuf_tensor("GB", [128, T], BF16))
        VB = e(nc.sbuf_tensor("VB", [128, T], BF16))

        pf = [e(nc.psum_tensor(f"pf{i}", [128, 512], F32)) for i in range(6)]
        pt0 = e(nc.psum_tensor("pt0", [128, 1024], BF16))
        pt1 = e(nc.psum_tensor("pt1", [128, 1024], BF16))

        attn_ctx = ExitStack()
        XN = attn_ctx.enter_context(nc.sbuf_tensor("XN", [128, DC * XST], BF16))
        KC = attn_ctx.enter_context(nc.sbuf_tensor("KC", [128, RC * T], BF16))
        HB = attn_ctx.enter_context(nc.sbuf_tensor("HB", [128, HCN * T], BF16))

        p0_ctx = ExitStack()
        p0f = p0_ctx.enter_context(nc.sbuf_tensor("p0f", [128, 1024], F32))
        p0b = p0_ctx.enter_context(nc.sbuf_tensor("p0b", [128, 1024], BF16))

        a_ctx = ExitStack()
        XT0 = a_ctx.enter_context(nc.sbuf_tensor("XT0", [128, D], F32))
        XT1 = a_ctx.enter_context(nc.sbuf_tensor("XT1", [128, D], F32))
        XT2 = a_ctx.enter_context(nc.sbuf_tensor("XT2", [128, D], F32))
        VBa = a_ctx.enter_context(nc.sbuf_tensor("VBa", [128, T], BF16))

        def xn_tok(dc, a, w):
            return XN[:, dc * XST + 2 + a: dc * XST + 2 + a + w]

        def v3(tile, q, width=None):
            ap = tile[:, :] if width is None else tile[:, :width]
            return ap.rearrange("p (a q) -> p a q", q=q)

        P.set_phase("ph_const")
        CONST_DM = 0
        for dst, src in [(c_bu, bu_i), (c_bv, bv_i), (c_bg1, bg1_i), (c_bg2, bg2_i),
                         (c_cw, cw_i), (c_cb, cb_i), (c_su, su_i), (c_sud, sud_i),
                         (c_wsu, wsu_i), (c_wsg, wsg_i), (c_wsd, wsd_i), (c_mask, mask_i),
                         (c_id, id_i), (c_h128, h128_i)]:
            P.dma("sync", dst[:, :], src[:, :])
        CONST_BARRIER = P.queue_barrier("sync")
        P.emit("vector", lambda v: v.memset(c_ones[:, :], 1.0))
        P.emit("vector", lambda v: v.memset(c_eps[:, :], 1e-5))
        P.emit("vector", lambda v: v.tensor_scalar(out=c_su[:, :], in0=c_su[:, :],
                                                   scalar1=RSQ_D, scalar2=None, op0=OP.mult),
               wait=CONST_BARRIER)
        SU_DV = P.emit("vector", lambda v: v.tensor_scalar(out=c_wsd[:, :], in0=c_wsd[:, :],
                                                           scalar1=RSQ_I, scalar2=None, op0=OP.mult),
                       inc="dv")
        for dc in range(DC):
            P.emit("vector", lambda v, dc=dc: v.tensor_scalar(
                out=c_hsu[:, dc * 128:(dc + 1) * 128], in0=c_h128[:, :],
                scalar1=c_su[:, dc:dc + 1], scalar2=None, op0=OP.mult),
                wait=[("dv", SU_DV)] if dc == 0 else ())
        CONST_DV = P.emit("vector", lambda v: v.tensor_copy(out=sprt[:, 0:1], in_=c_su[:, 0:1]),
                          inc="dv")

        P.set_phase("ph_p0cast")
        def ag(kind_in, kind_out):
            def fn(g):
                return g.collective_compute("AllGather", OP.bypass,
                                            replica_groups=[list(range(NCORES))],
                                            ins=[kind_in.ap().opt()],
                                            outs=[kind_out.ap().opt()])
            return fn

        p0_state = {"cast": 0, "st": None}

        def p0_cast(src_view, dst_view, q, queue="gpsimd"):
            ld = P.dma(queue, v3(p0f, q), src_view,
                       wait=[("ac", p0_state["cast"])])
            wcast = [ld]
            if p0_state["st"] is not None:
                wcast.append(p0_state["st"])
            cast = P.emit("scalar", lambda sc: sc.activation(out=p0b[:, :], in_=p0f[:, :],
                                                             func=AF.Copy),
                          wait=wcast, inc="ac")
            p0_state["cast"] = cast
            st = P.dma(queue, dst_view, v3(p0b, q), wait=[("ac", cast)])
            p0_state["st"] = st
            return st

        for j in range(2):
            for hh in range(2):
                st = p0_cast(wv_sl[j, hh * 8:(hh + 1) * 8].rearrange("dc p q -> p dc q"),
                             ag_wv_in[j, hh * 8:(hh + 1) * 8].rearrange("dc p q -> p dc q"), 128)
        CC_WV = P.emit("gpsimd", ag(ag_wv_in, ag_wv), wait=P.queue_barrier("gpsimd"), inc="cc")
        for j in range(4):
            st = p0_cast(wuv_sl[j * 2:(j + 1) * 2].rearrange("b p q -> p b q"),
                         ag_wuv_in[j * 2:(j + 1) * 2].rearrange("b p q -> p b q"), 512)
        CC_WUV = P.emit("gpsimd", ag(ag_wuv_in, ag_wuv), wait=P.queue_barrier("gpsimd"), inc="cc")
        P0_AC = P.n("ac")

        P.set_phase("ph_A")
        def ln_stats(xt, col, psrc, w_red, w_sq):
            red = P.emit("vector", lambda v: v.tensor_reduce(
                out=st_s1[:psrc, col:col + 1], in_=xt[:psrc, :],
                axis=mybir.AxisListType.X, op=OP.add), wait=w_red, inc="dv")
            P.emit("vector", lambda v: v.tensor_scalar(
                out=st_m[:psrc, col:col + 1], in0=st_s1[:psrc, col:col + 1],
                scalar1=1.0 / D, scalar2=None, op0=OP.mult), wait=[("dv", red)])
            negm = P.emit("vector", lambda v: v.tensor_scalar(
                out=st_n[:psrc, col:col + 1], in0=st_s1[:psrc, col:col + 1],
                scalar1=-1.0 / D, scalar2=None, op0=OP.mult), inc="dv")
            sq = P.emit("scalar", lambda sc: sc.activation(
                out=GB[:psrc, :], in_=xt[:psrc, :], func=AF.Square,
                bias=st_n[:psrc, col:col + 1],
                accum_out=st_s2[:psrc, col:col + 1]),
                wait=w_sq + [("dv", negm)], inc="ac")
            sqr = P.emit("scalar", lambda sc: sc.activation(
                out=st_t[:psrc, col:col + 1], in_=st_s2[:psrc, col:col + 1],
                func=AF.Sqrt, scale=1.0 / D, bias=c_eps[:psrc, 0:1]),
                wait=[("ac", sq)], inc="ac")
            return sq, sqr

        def ln_finish(xt, col, dst, psrc, sqr, w_ap):
            recip = P.emit("vector", lambda v: v.reciprocal(
                out=st_r[:psrc, col:col + 1], in_=st_t[:psrc, col:col + 1]),
                wait=[("ac", sqr)], inc="dv")
            apply = P.emit("vector", lambda v: v.tensor_scalar(
                out=dst[:psrc, :], in0=xt[:psrc, :],
                scalar1=st_m[:psrc, col:col + 1], scalar2=st_r[:psrc, col:col + 1],
                op0=OP.subtract, op1=OP.mult),
                wait=w_ap + [("dv", recip)], inc="dv")
            return apply

        def ln_tile(xt, col, dst, psrc, kind):
            red = P.emit("vector", lambda v: v.tensor_reduce(
                out=st_s1[:psrc, col:col + 1], in_=xt[:psrc, :],
                axis=mybir.AxisListType.X, op=OP.add), wait=kind["w_red"], inc="dv")
            P.emit("vector", lambda v: v.tensor_scalar(
                out=st_m[:psrc, col:col + 1], in0=st_s1[:psrc, col:col + 1],
                scalar1=1.0 / D, scalar2=None, op0=OP.mult), wait=[("dv", red)])
            negm = P.emit("vector", lambda v: v.tensor_scalar(
                out=st_n[:psrc, col:col + 1], in0=st_s1[:psrc, col:col + 1],
                scalar1=-1.0 / D, scalar2=None, op0=OP.mult), inc="dv")
            sq = P.emit("scalar", lambda sc: sc.activation(
                out=GB[:psrc, :], in_=xt[:psrc, :], func=AF.Square,
                bias=st_n[:psrc, col:col + 1],
                accum_out=st_s2[:psrc, col:col + 1]),
                wait=kind["w_sq"] + [("dv", negm)], inc="ac")
            sqr = P.emit("scalar", lambda sc: sc.activation(
                out=st_t[:psrc, col:col + 1], in_=st_s2[:psrc, col:col + 1],
                func=AF.Sqrt, scale=1.0 / D, bias=c_eps[:psrc, 0:1]),
                wait=[("ac", sq)], inc="ac")
            recip = P.emit("vector", lambda v: v.reciprocal(
                out=st_r[:psrc, col:col + 1], in_=st_t[:psrc, col:col + 1]),
                wait=[("ac", sqr)], inc="dv")
            apply = P.emit("vector", lambda v: v.tensor_scalar(
                out=dst[:psrc, :], in0=xt[:psrc, :],
                scalar1=st_m[:psrc, col:col + 1], scalar2=st_r[:psrc, col:col + 1],
                op0=OP.subtract, op1=OP.mult),
                wait=kind["w_ap"] + [("dv", recip)], inc="dv")
            return sq, apply

        a_dma = {}
        a_sq = {}
        a_sqr = {}
        a_apply = {}
        a_tr = {}
        a_ev = {}
        XTs = [XT0, XT1, XT2]
        VBs = [VB, VBa]

        def a_finish(t):
            xt = XTs[t % 3]
            vb = VBs[t % 2]
            w_ap = [("pe", a_tr[t - 2][1])] if t >= 2 else []
            a_apply[t] = ln_finish(xt, t, vb, 128, a_sqr[t], w_ap)
            for half, pt in [(0, pt0), (1, pt1)]:
                wtr = [("dv", a_apply[t])]
                if t >= 1:
                    wtr.append(("ac", a_ev[t - 1][half]))
                last = None
                for j in range(8):
                    dc = half * 8 + j
                    last = P.emit("tensor", lambda te, pt=pt, j=j, dc=dc, vb=vb:
                                  te.transpose(
                                      out=pt[:, j * 128:(j + 1) * 128],
                                      in_=vb[:, dc * 128:(dc + 1) * 128],
                                      identity=c_id[:, :]),
                                  wait=wtr if j == 0 else (),
                                  inc="pe" if j == 7 else None)
                a_tr.setdefault(t, {})[half] = last
            for half, pt in [(0, pt0), (1, pt1)]:
                dst = XN[:, :].rearrange("p (dc q) -> p dc q", q=XST)[
                    :, half * 8:(half + 1) * 8, 2 + t * 128: 2 + (t + 1) * 128]
                a_ev.setdefault(t, {})[half] = P.emit(
                    "scalar", lambda sc, pt=pt, dst=dst: sc.activation(
                        out=dst, in_=pt[:, :].rearrange("p (dc q) -> p dc q", q=128),
                        func=AF.Copy),
                    wait=[("pe", a_tr[t][half])], inc="ac")

        for t in range(NL):
            xt = XTs[t % 3]
            w = list(CONST_BARRIER) if t == 0 else []
            if t >= 3:
                w = [("dv", a_apply[t - 3]), ("ac", a_sq[t - 3])]
            a_dma[t] = P.dma("sync", xt[:, :], xs[1 + t * 128: 1 + (t + 1) * 128, :], wait=w)
            a_sq[t], a_sqr[t] = ln_stats(xt, t, 128, [a_dma[t]], [a_dma[t]])
            if t >= 1:
                a_finish(t - 1)
        a_finish(NL - 1)
        hd0 = P.dma("sync", XT0[0:1, :], xs[0:1, :],
                    wait=[("dv", a_apply[15]), ("ac", a_sq[15])])
        hd1 = P.dma("sync", XT0[1:2, :], xs[T + 1:T + 2, :])
        kind = {"w_sq": [hd0, hd1], "w_red": [hd0, hd1],
                "w_ap": [("pe", a_tr[15][1])]}
        h_sq, h_apply = ln_tile(XT0, 0, VB, 2, kind)
        last = None
        for dc in range(DC):
            last = P.emit("tensor", lambda te, dc=dc: te.transpose(
                out=pt0[:, dc * 2:(dc + 1) * 2], in_=VB[0:2, dc * 128:(dc + 1) * 128],
                identity=c_id[0:2, 0:2]),
                wait=[("dv", h_apply), ("ac", a_ev[15][0])] if dc == 0 else (),
                inc="pe" if dc == DC - 1 else None)
        halo_tr = last
        xnv = XN[:, :].rearrange("p (dc q) -> p dc q", q=XST)
        ptv = pt0[:, 0:32].rearrange("p (dc q) -> p dc q", q=2)
        P.emit("scalar", lambda sc: sc.activation(
            out=xnv[:, :, 1:2], in_=ptv[:, :, 0:1], func=AF.Copy),
            wait=[("pe", halo_tr)], inc="ac")
        A_AC_END = P.emit("scalar", lambda sc: sc.activation(
            out=xnv[:, :, 2050:2051], in_=ptv[:, :, 1:2], func=AF.Copy), inc="ac")
        A_DV_END = h_apply
        A_PE_END = halo_tr

        if "xn" in taps:
            P.dma("sync", dbg["xn"], XN[:, :], wait=[("ac", A_AC_END)])

        a_ctx.close()


        wdn_ctx = ExitStack()
        wd1 = wdn_ctx.enter_context(nc.sbuf_tensor("wd1", [128, 1024], F32))
        wd2 = wdn_ctx.enter_context(nc.sbuf_tensor("wd2", [128, 1024], F32))
        wdt = wdn_ctx.enter_context(nc.sbuf_tensor("wdt", [128, 512], F32))
        wdc = wdn_ctx.enter_context(nc.sbuf_tensor("wdc", [128, 1024], BF16))
        wds = wdn_ctx.enter_context(nc.sbuf_tensor("wds", [128, 1024], BF16))
        wd3 = p0f
        strm_ctx = ExitStack()
        GBb = strm_ctx.enter_context(nc.sbuf_tensor("GBb", [128, T], BF16))
        wfp = strm_ctx.enter_context(nc.sbuf_tensor("wfp", [128, D], F32))
        wb0 = strm_ctx.enter_context(nc.sbuf_tensor("wb0", [128, D], BF16))
        wb1 = strm_ctx.enter_context(nc.sbuf_tensor("wb1", [128, D], BF16))
        wg2f = strm_ctx.enter_context(nc.sbuf_tensor("wg2f", [128, 512], F32))
        wg2b0 = strm_ctx.enter_context(nc.sbuf_tensor("wg2b0", [128, 512], BF16))
        wg2b1 = strm_ctx.enter_context(nc.sbuf_tensor("wg2b1", [128, 512], BF16))
        A_FENCE = [("dv", A_DV_END), ("ac", A_AC_END), ("pe", A_PE_END)]

        P.set_phase("ph_h")
        wb = [wb0, wb1]
        h_cast = {}
        h_mm = {}
        h_ev = {}
        for hc in range(HCN):
            w = list(A_FENCE) if hc == 0 else [("dv", h_cast[hc - 1])]
            ld = P.dma("sync", v3(wfp, 128), wg1[hc].rearrange("dc p q -> p dc q"), wait=w)
            wcast = [("dm", ld)]
            if hc >= 2:
                wcast.append(("pe", h_mm[hc - 2][3]))
            h_cast[hc] = P.emit("vector", lambda v, hc=hc: v.tensor_scalar(
                out=wb[hc % 2][:, :], in0=wfp[:, :], scalar1=1.0, scalar2=None, op0=OP.mult),
                wait=wcast, inc="dv")
            for lt in range(4):
                wmm = [("dv", h_cast[hc])] if lt == 0 else []
                if hc >= 1:
                    wmm.append(("ac", h_ev[hc - 1][lt]))
                last = None
                for dc in range(DC):
                    last = P.emit("tensor", lambda te, hc=hc, lt=lt, dc=dc: te.matmul(
                        out=pf[lt][:, :], lhsT=wb[hc % 2][:, dc * 128:(dc + 1) * 128],
                        rhs=xn_tok(dc, lt * 512, 512),
                        start=(dc == 0), stop=(dc == DC - 1)),
                        wait=wmm if dc == 0 else (), inc="pe" if dc == DC - 1 else None)
                h_mm.setdefault(hc, {})[lt] = last
                h_ev.setdefault(hc, {})[lt] = P.emit("scalar", lambda sc, hc=hc, lt=lt:
                    sc.activation(out=HB[:, hc * T + lt * 512: hc * T + (lt + 1) * 512],
                                  in_=pf[lt][:, :], func=AF.Relu,
                                  bias=c_bg1[:, hc:hc + 1]),
                    wait=[("pe", h_mm[hc][lt])], inc="ac")
        H_DONE_AC = h_ev[HCN - 1][3]
        if "hb" in taps:
            P.dma("sync", dbg["hb"], HB[:, :], wait=[("ac", H_DONE_AC)])

        P.set_phase("ph_B")
        wg2b = [wg2b0, wg2b1]

        def g_block(rc, phase, prev_gmm, prev_cast, gb_war, gev_war, gb):
            w = [("dv", prev_cast)] if prev_cast else list(A_FENCE)
            ld = P.dma("sync", v3(wg2f, 128), wg2[rc].rearrange("hc p q -> p hc q"), wait=w)
            wcast = [("dm", ld)]
            if prev_gmm is not None:
                wcast.append(("pe", prev_gmm[3]))
            cast = P.emit("vector", lambda v, rc=rc: v.tensor_scalar(
                out=wg2b[rc % 2][:, :], in0=wg2f[:, :], scalar1=1.0, scalar2=None,
                op0=OP.mult), wait=wcast, inc="dv")
            gmm = {}
            gev = {}
            for lt in range(4):
                bank = pf[4 + lt % 2]
                wmm = [("dv", cast)] if lt == 0 else []
                if lt == 0 and phase == "B" and rc == 0:
                    wmm.append(("ac", H_DONE_AC))
                if lt >= 2:
                    wmm.append(("ac", gev[lt - 2]))
                elif gev_war is not None:
                    wmm.append(("ac", gev_war[lt + 2]))
                last = None
                for hc in range(HCN):
                    last = P.emit("tensor", lambda te, rc=rc, lt=lt, hc=hc, bank=bank:
                                  te.matmul(out=bank[:, :],
                                            lhsT=wg2b[rc % 2][:, hc * 128:(hc + 1) * 128],
                                            rhs=HB[:, hc * T + lt * 512: hc * T + (lt + 1) * 512],
                                            start=(hc == 0), stop=(hc == HCN - 1)),
                                  wait=wmm if hc == 0 else (),
                                  inc="pe" if hc == HCN - 1 else None)
                gmm[lt] = last
                wev = [("pe", gmm[lt])]
                if gb_war is not None and lt == 0:
                    wev.append(("dv", gb_war))
                gev[lt] = P.emit("scalar", lambda sc, rc=rc, lt=lt, bank=bank, gb=gb:
                                 sc.activation(out=gb[:, lt * 512:(lt + 1) * 512],
                                               in_=bank[:, :], func=AF.Sigmoid,
                                               bias=c_bg2[:, rc:rc + 1]),
                                 wait=wev, inc="ac")
            return cast, gmm, gev

        b_cast = {}
        b_gmm = {}
        b_gev = {}
        b_wu_cast = {}
        b_umm = {}
        b_ugev = {}
        b_scan = {}
        for rc in range(RC):
            gb = GB if rc % 2 == 0 else GBb
            b_cast[rc], b_gmm[rc], b_gev[rc] = g_block(
                rc, "B",
                prev_gmm=b_gmm.get(rc - 1),
                prev_cast=b_cast.get(rc - 1),
                gb_war=b_scan.get(rc - 2),
                gev_war=b_gev.get(rc - 1), gb=gb)
            w = [("dv", b_wu_cast[rc - 1])] if rc > 0 else \
                list(A_FENCE) + [("dv", h_cast[HCN - 1])]
            ld = P.dma("sync", v3(wfp, 128), wu_u[rc].rearrange("dc p q -> p dc q"), wait=w)
            wcast = [("dm", ld)]
            if rc >= 2:
                wcast.append(("pe", b_umm[rc - 2][3]))
            else:
                wcast.append(("pe", h_mm[HCN - 1][3]))
            b_wu_cast[rc] = P.emit("vector", lambda v, rc=rc: v.tensor_scalar(
                out=wb[rc % 2][:, :], in0=wfp[:, :], scalar1=1.0, scalar2=None, op0=OP.mult),
                wait=wcast, inc="dv")
            for lt in range(4):
                wmm = [("dv", b_wu_cast[rc])] if lt == 0 else []
                if rc >= 1:
                    wmm.append(("dv", b_ugev[rc - 1][lt]))
                else:
                    wmm.append(("ac", h_ev[HCN - 1][lt]))
                last = None
                for dc in range(DC):
                    last = P.emit("tensor", lambda te, rc=rc, lt=lt, dc=dc: te.matmul(
                        out=pf[lt][:, :], lhsT=wb[rc % 2][:, dc * 128:(dc + 1) * 128],
                        rhs=xn_tok(dc, lt * 512, 512),
                        start=(dc == 0), stop=(dc == DC - 1)),
                        wait=wmm if dc == 0 else (), inc="pe" if dc == DC - 1 else None)
                b_umm.setdefault(rc, {})[lt] = last
                b_ugev.setdefault(rc, {})[lt] = P.emit(
                    "vector", lambda v, rc=rc, lt=lt, gb=gb: v.scalar_tensor_tensor(
                        out=gb[:, lt * 512:(lt + 1) * 512], in0=pf[lt][:, :],
                        scalar=c_bu[:, rc:rc + 1],
                        in1=gb[:, lt * 512:(lt + 1) * 512],
                        op0=OP.add, op1=OP.mult),
                    wait=[("pe", b_umm[rc][lt]), ("ac", b_gev[rc][lt])], inc="dv")

            for slot in range(2):
                pidx = rc * 2 + slot
                wsl, agin = (wup_sl, ag_wup_in) if pidx < 16 else (wgt_sl, ag_wgt_in)
                pj = pidx % 16
                p0_cast(wsl[pj // 4, (pj % 4) * 4:(pj % 4) * 4 + 4]
                        .rearrange("dc p q -> p dc q"),
                        agin[pj // 4, (pj % 4) * 4:(pj % 4) * 4 + 4]
                        .rearrange("dc p q -> p dc q"), 256, queue="sync")
            scn = P.emit("vector", lambda v, rc=rc, gb=gb: v.tensor_tensor_scan(
                out=KC[:, rc * T:(rc + 1) * T], data0=c_ones[:, :], data1=gb[:, :],
                initial=0.0, op0=OP.mult, op1=OP.add), inc="dv")
            b_scan[rc] = P.emit("vector", lambda v, rc=rc: v.tensor_copy(
                out=sfm[:, rc:rc + 1], in_=KC[:, rc * T + T - 1: rc * T + T]),
                wait=[("dv", scn)], inc="dv")
        B_SCAN_END = b_scan[RC - 1]
        if "kc" in taps:
            P.dma("sync", dbg["kc"], KC[:, :], wait=[("dv", B_SCAN_END)])

        P.set_phase("ph_wdn")
        wdn_state = {"tail": None, "cast": None, "ptwar": None}

        def wdn_butterflies(dcn):
            st_prev = None
            for strip in range(8):
                w = []
                if dcn == 0 and strip == 0:
                    w = list(A_FENCE)
                if st_prev is not None:
                    w.append(st_prev)
                ld = P.dma("gpsimd", wd1[:, :],
                           wdn_sl[dcn][:, strip * 1024:(strip + 1) * 1024], wait=w)
                first = True
                for hs in range(10):
                    h = 1 << hs
                    nb = 1024 // (2 * h)
                    vv = wd1[:, :].rearrange("p (nb two h) -> p nb two h", two=2, h=h)
                    a_, b_ = vv[:, :, 0, :], vv[:, :, 1, :]
                    tmp = wdt[:, :].rearrange("p (nb h) -> p nb h", h=h)[:, 0:nb, :]
                    P.emit("gpsimd", lambda g, a_=a_, b_=b_, tmp=tmp: g.tensor_tensor(
                        out=tmp, in0=a_, in1=b_, op=OP.subtract),
                        wait=[ld] if first else ())
                    first = False
                    P.emit("gpsimd", lambda g, a_=a_, b_=b_: g.tensor_tensor(
                        out=a_, in0=a_, in1=b_, op=OP.add))
                    P.emit("gpsimd", lambda g, b_=b_, tmp=tmp: g.tensor_copy(
                        out=b_, in_=tmp))
                st_prev = P.dma("gpsimd", wdn_b[dcn][:, strip * 1024:(strip + 1) * 1024],
                                wd1[:, :])
            pairs = {1024: [(0, 1), (2, 3), (4, 5), (6, 7)],
                     2048: [(0, 2), (1, 3), (4, 6), (5, 7)],
                     4096: [(0, 4), (1, 5), (2, 6), (3, 7)]}
            st_a = [st_prev]
            for h in [1024, 2048, 4096]:
                for (ia, ib) in pairs[h]:
                    lda = P.dma("gpsimd", wd1[:, :], wdn_b[dcn][:, ia * 1024:(ia + 1) * 1024],
                                wait=list(st_a))
                    ldb = P.dma("gpsimd", wd2[:, :], wdn_b[dcn][:, ib * 1024:(ib + 1) * 1024])
                    pl_pair = None
                    for hh2 in range(2):
                        av = wd1[:, hh2 * 512:(hh2 + 1) * 512]
                        bv = wd2[:, hh2 * 512:(hh2 + 1) * 512]
                        P.emit("gpsimd", lambda g, av=av, bv=bv: g.tensor_tensor(
                            out=wdt[:, :], in0=av, in1=bv, op=OP.subtract),
                            wait=[lda, ldb] if hh2 == 0 else ())
                        P.emit("gpsimd", lambda g, av=av, bv=bv: g.tensor_tensor(
                            out=av, in0=av, in1=bv, op=OP.add))
                        pl_pair = P.emit("gpsimd", lambda g, bv=bv: g.tensor_copy(
                            out=bv, in_=wdt[:, :]), inc="pl" if hh2 == 1 else None)
                    sta = P.dma("gpsimd", wdn_b[dcn][:, ia * 1024:(ia + 1) * 1024],
                                wd1[:, :], wait=[("pl", pl_pair)])
                    stb = P.dma("gpsimd", wdn_b[dcn][:, ib * 1024:(ib + 1) * 1024],
                                 wd2[:, :])
                    st_a = [sta, stb]
            wdn_state[dcn] = st_a

        def wdn_tail_piece(dcn, k):
            wld = []
            if wdn_state["cast"] is not None:
                wld.append(("ac", wdn_state["cast"]))
            else:
                wld.append(("ac", p0_state["cast"]))
                if p0_state["st"] is not None:
                    wld.append(p0_state["st"])
            if k == 0:
                wld += list(wdn_state[dcn])
            ld = P.dma("sync", wd3[:, :], wdn_b[dcn][:, k * 1024:(k + 1) * 1024],
                       wait=wld)
            cast = P.emit("scalar", lambda sc, dcn=dcn: sc.activation(
                out=wdc[:, :], in_=wd3[:, :], func=AF.Copy,
                scale=c_wsd[:, dcn:dcn + 1]), wait=[ld, ("dv", SU_DV)], inc="ac")
            wdn_state["cast"] = cast
            wpt = [("ac", cast)]
            if wdn_state["ptwar"] is not None:
                wpt.append(("ac", wdn_state["ptwar"]))
            else:
                wpt.append(("pe", A_PE_END))
            last = None
            for jj in range(8):
                last = P.emit("tensor", lambda te, jj=jj: te.transpose(
                    out=pt0[:, jj * 128:(jj + 1) * 128],
                    in_=wdc[:, jj * 128:(jj + 1) * 128], identity=c_id[:, :]),
                    wait=wpt if jj == 0 else (),
                    inc="pe" if jj == 7 else None)
            evl = None
            for jj in range(8):
                jc = k * 8 + jj
                wev = []
                if jj == 0:
                    wev = [("pe", last)]
                    if wdn_state["tail"] is not None:
                        wev.append(wdn_state["tail"])
                evl = P.emit("scalar", lambda sc, jj=jj, jc=jc: sc.activation(
                    out=wds[:, jj * 128:(jj + 1) * 128],
                    in_=pt0[:, jj * 128:(jj + 1) * 128], func=AF.Copy,
                    scale=c_sud[:, jc:jc + 1]),
                    wait=wev, inc="ac" if jj == 7 else None)
            wdn_state["ptwar"] = evl
            st = P.dma(
                "sync",
                ag_wdn_in[k * 8:(k + 1) * 8, :, dcn * 128:(dcn + 1) * 128]
                .rearrange("jc j q -> j jc q"),
                v3(wds, 128), wait=[("ac", evl)])
            wdn_state["tail"] = st
            return evl

        wdn_butterflies(0)

        P.set_phase("ph_C")
        sst = P.dma("gpsimd", ag_s_in.ap().rearrange("one (p q) -> (one p) q", p=128),
                    sfm[:, :], wait=[("dv", B_SCAN_END)])
        CC_S = P.emit("gpsimd", ag(ag_s_in, ag_s), wait=[("dg", sst)], inc="cc")
        sld = P.dma("gpsimd", sall[:, :].rearrange("p (r q) -> p r q", q=RC),
                    ag_s.ap().rearrange("r (p q) -> p r q", p=128), wait=[("cc", CC_S)])
        P.emit("vector", lambda v: v.memset(soff[:, :], 0.0), wait=[("dg", sld)])
        C_OFF = None
        for j in range(NCORES):
            C_OFF = P.emit("vector", lambda v, j=j: v.scalar_tensor_tensor(
                out=soff[:, :], in0=sall[:, j * RC:(j + 1) * RC],
                scalar=c_mask[:, j:j + 1], in1=soff[:, :], op0=OP.mult, op1=OP.add),
                inc="dv" if j == NCORES - 1 else None)
        if "soff" in taps:
            P.dma("sync", dbg["soff"][:, 0:NCORES * RC], sall[:, :], wait=[("dv", C_OFF)])
            P.dma("sync", dbg["soff"][:, NCORES * RC:], soff[:, :], wait=[("dv", C_OFF)])

        CC_WUP = P.emit("gpsimd", ag(ag_wup_in, ag_wup),
                        wait=P.queue_barrier("sync") + P.queue_barrier("gpsimd"), inc="cc")
        CC_WGT = P.emit("gpsimd", ag(ag_wgt_in, ag_wgt),
                        wait=P.queue_barrier("sync") + P.queue_barrier("gpsimd"), inc="cc")

        P.set_phase("ph_wdn2")
        wdn_butterflies(1)

        P.set_phase("ph_D")
        d_cast = {}
        d_gmm = {}
        d_gev = {}
        d_vld = {}
        d_vmm = {}
        d_vgev = {}
        d_go = {}
        for rc in range(RC):
            gb = GB if rc % 2 == 0 else GBb
            d_cast[rc], d_gmm[rc], d_gev[rc] = g_block(
                rc, "D",
                prev_gmm=d_gmm.get(rc - 1) or b_gmm[RC - 1],
                prev_cast=d_cast.get(rc - 1) or b_cast[RC - 1],
                gb_war=d_go.get(rc - 2) or B_SCAN_END,
                gev_war=d_gev.get(rc - 1) or b_gev[RC - 1], gb=gb)
            w = [("cc", CC_WV)]
            if rc >= 2:
                w.append(("pe", d_vmm[rc - 2][3]))
            else:
                w.append(("pe", b_umm[RC - 1][3]))
            d_vld[rc] = P.dma("sync", v3(wb[rc % 2], 128),
                              ag_wv[rc].rearrange("dc p q -> p dc q"), wait=w)
            for lt in range(4):
                wmm = [("dm", d_vld[rc])] if lt == 0 else []
                if rc >= 1:
                    wmm.append(("dv", d_vgev[rc - 1][lt]))
                else:
                    wmm.append(("dv", b_ugev[RC - 1][lt]))
                last = None
                for dc in range(DC):
                    last = P.emit("tensor", lambda te, rc=rc, lt=lt, dc=dc: te.matmul(
                        out=pf[lt][:, :], lhsT=wb[rc % 2][:, dc * 128:(dc + 1) * 128],
                        rhs=xn_tok(dc, lt * 512, 512),
                        start=(dc == 0), stop=(dc == DC - 1)),
                        wait=wmm if dc == 0 else (), inc="pe" if dc == DC - 1 else None)
                d_vmm.setdefault(rc, {})[lt] = last
                d_vgev.setdefault(rc, {})[lt] = P.emit(
                    "vector", lambda v, rc=rc, lt=lt, gb=gb: v.scalar_tensor_tensor(
                        out=VB[:, lt * 512:(lt + 1) * 512], in0=pf[lt][:, :],
                        scalar=c_bv[:, rc:rc + 1], in1=gb[:, lt * 512:(lt + 1) * 512],
                        op0=OP.add, op1=OP.mult),
                    wait=[("pe", d_vmm[rc][lt]), ("ac", d_gev[rc][lt])], inc="dv")
            if rc < 8:
                wdn_tail_piece(0, rc)
            wgo = [("dv", C_OFF)] if rc == 0 else []
            d_go[rc] = P.emit("vector", lambda v, rc=rc: v.scalar_tensor_tensor(
                out=KC[:, rc * T:(rc + 1) * T], in0=KC[:, rc * T:(rc + 1) * T],
                scalar=soff[:, rc:rc + 1], in1=VB[:, :], op0=OP.add, op1=OP.mult),
                wait=wgo, inc="dv")
        D_GO_END = d_go[RC - 1]
        D_PE_END = d_vmm[RC - 1][3]
        if "go" in taps:
            P.dma("sync", dbg["go"], KC[:, :], wait=[("dv", D_GO_END)])

        strm_ctx.close()

        e_ctx = ExitStack()
        wuvb = e_ctx.enter_context(nc.sbuf_tensor("wuvb", [128, 16 * 512], BF16))
        ext = [e_ctx.enter_context(nc.sbuf_tensor(f"ext{i}", [128, 512], F32)) for i in range(2)]
        etb = [e_ctx.enter_context(nc.sbuf_tensor(f"etb{i}", [128, 512], F32)) for i in range(2)]
        E_FENCE = [("pe", D_PE_END), ("dv", D_GO_END), ("ac", P.n("ac"))] \
            + P.queue_barrier("sync")

        P.set_phase("ph_E")
        ttm = HB
        e_uvld = {}
        e_conv = {}
        e_tev = {}
        e_uvmm = {}
        e_easm = {}
        e_eadd = {}
        e_est = {}
        for dt in range(4):
            w = [("cc", CC_WUV)] + (list(E_FENCE) if dt == 0 else [("pe", e_uvmm[(dt - 1, 15)])])
            e_uvld[dt] = P.dma("sync", v3(wuvb, 512),
                               ag_wuv[dt * 16:(dt + 1) * 16].rearrange("b p q -> p b q"),
                               wait=w)
            for q in range(4):
                dc = dt * 4 + q
                base = dc * XST
                xm1 = XN[:, base + 1: base + 1 + T]
                x0 = XN[:, base + 2: base + 2 + T]
                xp1 = XN[:, base + 3: base + 3 + T]
                wcv = []
                if dt == 0 and q == 0:
                    wcv = [("ac", d_gev[RC - 1][3]), ("dv", D_GO_END)]
                P.emit("vector", lambda v, xm1=xm1, dc=dc: v.tensor_scalar(
                    out=GB[:, :], in0=xm1, scalar1=c_cw[:, dc * 3: dc * 3 + 1],
                    scalar2=c_cb[:, dc:dc + 1], op0=OP.mult, op1=OP.add), wait=wcv)
                P.emit("vector", lambda v, x0=x0, dc=dc: v.scalar_tensor_tensor(
                    out=GB[:, :], in0=x0, scalar=c_cw[:, dc * 3 + 1: dc * 3 + 2],
                    in1=GB[:, :], op0=OP.mult, op1=OP.add))
                e_conv[(dt, q)] = P.emit("vector", lambda v, xp1=xp1, dc=dc:
                    v.scalar_tensor_tensor(
                        out=GB[:, :], in0=xp1, scalar=c_cw[:, dc * 3 + 2: dc * 3 + 3],
                        in1=GB[:, :], op0=OP.mult, op1=OP.add), inc="dv")
                for half, pt in [(0, pt0), (1, pt1)]:
                    wtr = [("dv", e_conv[(dt, q)])]
                    if dt == 0 and q == 0:
                        wtr.append(("ac", P.n("ac")))
                    else:
                        prev = e_tev.get((dt, q - 1), e_tev.get((dt - 1, 3)))
                        wtr.append(("ac", prev[half]))
                        if half == 0 and wdn_state["ptwar"] is not None:
                            wtr.append(("ac", wdn_state["ptwar"]))
                    last = None
                    for j in range(8):
                        tl = half * 8 + j
                        last = P.emit("tensor", lambda te, pt=pt, j=j, tl=tl: te.transpose(
                            out=pt[:, j * 128:(j + 1) * 128],
                            in_=GB[:, tl * 128:(tl + 1) * 128], identity=c_id[:, :]),
                            wait=wtr if j == 0 else (), inc="pe" if j == 7 else None)
                    e_conv[(dt, q, half)] = last
                tev = {}
                for half, pt in [(0, pt0), (1, pt1)]:
                    dst = ttm[:, :].rearrange("p (tl q) -> p tl q", q=512)[
                        :, half * 8:(half + 1) * 8, q * 128:(q + 1) * 128]
                    tev[half] = P.emit("scalar", lambda sc, pt=pt, dst=dst: sc.activation(
                        out=dst, in_=pt[:, :].rearrange("p (tl q) -> p tl q", q=128),
                        func=AF.Copy),
                        wait=[("pe", e_conv[(dt, q, half)])], inc="ac")
                e_tev[(dt, q)] = tev
            wdn_tail_piece(1, dt * 2)
            wdn_tail_piece(1, dt * 2 + 1)
            for l in range(NL):
                wmm = []
                if l == 0:
                    wmm = [("dm", e_uvld[dt])]
                if dt == 0 and l < 4:
                    wmm.append(("dv", D_GO_END))
                if (dt, l - 4) in e_easm:
                    wmm.append(("dv", e_easm[(dt, l - 4)]))
                elif dt > 0 and (dt - 1, NL - 4 + l) in e_easm:
                    wmm.append(("dv", e_easm[(dt - 1, NL - 4 + l)]))
                last = None
                for rc in range(RC):
                    last = P.emit("tensor", lambda te, dt=dt, l=l, rc=rc: te.matmul(
                        out=pf[l % 4][:, :],
                        lhsT=KC[:, rc * T + l * 128: rc * T + (l + 1) * 128],
                        rhs=wuvb[:, rc * 512:(rc + 1) * 512],
                        start=(rc == 0), stop=(rc == RC - 1)),
                        wait=wmm if rc == 0 else (), inc="pe" if rc == RC - 1 else None)
                e_uvmm[(dt, l)] = last
                wx = []
                if (dt, l - 2) in e_est:
                    wx.append(e_est[(dt, l - 2)])
                elif dt > 0:
                    wx.append(e_est[(dt - 1, NL - 2 + l)])
                else:
                    wx += list(E_FENCE)
                xld = P.dma("sync", ext[l % 2][:, :],
                            xs[1 + l * 128: 1 + (l + 1) * 128, dt * 512:(dt + 1) * 512],
                            wait=wx)
                wasm = [("pe", e_uvmm[(dt, l)]),
                        ("ac", e_tev[(dt, 3)][0]), ("ac", e_tev[(dt, 3)][1])]
                if (dt, l - 2) in e_eadd:
                    wasm.append(("pl", e_eadd[(dt, l - 2)]))
                elif dt > 0 and (dt - 1, NL - 2 + l) in e_eadd:
                    wasm.append(("pl", e_eadd[(dt - 1, NL - 2 + l)]))
                e_easm[(dt, l)] = P.emit("vector", lambda v, dt=dt, l=l: v.tensor_tensor(
                    out=etb[l % 2][:, :], in0=pf[l % 4][:, :],
                    in1=ttm[:, l * 512:(l + 1) * 512], op=OP.add),
                    wait=wasm, inc="dv")
                wadd = [("dv", e_easm[(dt, l)]), xld]
                e_eadd[(dt, l)] = P.emit("gpsimd", lambda g, dt=dt, l=l: g.tensor_tensor(
                    out=ext[l % 2][:, :], in0=etb[l % 2][:, :], in1=ext[l % 2][:, :],
                    op=OP.add), wait=wadd, inc="pl")
                e_est[(dt, l)] = P.dma("gpsimd",
                                       x1d[l * 128:(l + 1) * 128, dt * 512:(dt + 1) * 512],
                                       ext[l % 2][:, :], wait=[("pl", e_eadd[(dt, l)])])
        CC_WDN = P.emit("gpsimd", ag(ag_wdn_in, ag_wdn),
                        wait=P.queue_barrier("sync") + P.queue_barrier("gpsimd"), inc="cc")
        if "wdn" in taps:
            P.dma("sync", dbg["wdn"], ag_wdn_in.ap(), wait=[("cc", CC_WDN)])
        E_PE_END = e_uvmm[(3, NL - 1)]
        E_DV_END = e_easm[(3, NL - 1)]
        E_ST_END = e_est[(3, NL - 1)]
        if "x1" in taps:
            P.dma("gpsimd", dbg["x1"], x1d.ap(), wait=[("dg", E_ST_END)])

        e_ctx.close()
        wdn_ctx.close()
        p0_ctx.close()
        attn_ctx.close()

        f_ctx = ExitStack()
        XB = f_ctx.enter_context(nc.sbuf_tensor("XB", [128, DC * 1024], BF16))
        ACC = f_ctx.enter_context(nc.sbuf_tensor("ACC", [128, 8 * 2048], F32))
        YR = [f_ctx.enter_context(nc.sbuf_tensor(f"YR{i}", [128, 1024], BF16)) for i in range(8)]
        wupb = [f_ctx.enter_context(nc.sbuf_tensor(f"wupb{i}", [128, DC * 256], BF16)) for i in range(2)]
        wgtb = [f_ctx.enter_context(nc.sbuf_tensor(f"wgtb{i}", [128, DC * 256], BF16)) for i in range(2)]
        wdng = [f_ctx.enter_context(nc.sbuf_tensor(f"wdng{i}", [128, 8 * 512], BF16)) for i in range(2)]
        gsb = f_ctx.enter_context(nc.sbuf_tensor("gsb", [128, 1024], BF16))
        xft = [f_ctx.enter_context(nc.sbuf_tensor(f"xft{i}", [128, D], F32)) for i in range(2)]
        VB2 = f_ctx.enter_context(nc.sbuf_tensor("VB2", [128, T], BF16))
        opb = [f_ctx.enter_context(nc.sbuf_tensor(f"opb{i}", [128, 512], F32)) for i in range(2)]
        F_FENCE = [("pe", E_PE_END), ("dv", E_DV_END), ("ac", P.n("ac")),
                   ("pl", P.n("pl"))] \
            + P.queue_barrier("gpsimd") + P.queue_barrier("sync")

        vbs = [VB, VB2]
        f_xld = {}
        f_sq = {}
        f_sqr = {}
        f_ap = {}
        f_tr = {}
        f_ev = {}
        g_upmm = {}
        g_gtmm = {}
        g_gsev = {}
        g_ystt = {}
        g_dnmm = {}
        g_flush = {}
        g_wld = {}
        g_dnld = {}
        g_ost = {}
        g_oasm = {}
        XB_READY = {}

        for hf in range(2):
            P.set_phase(f"ph_F{hf}")
            def f_finish(t, hf=hf):
                vb = vbs[t % 2]
                w_ap = []
                if (hf, t - 2) in f_tr:
                    w_ap = [("pe", f_tr[(hf, t - 2)][1])]
                elif hf == 1:
                    w_ap = [("pe", f_tr[(0, 6 + t)][1])]
                f_ap[(hf, t)] = ln_finish(xft[t % 2], t, vb, 128, f_sqr[(hf, t)], w_ap)
                for half, pt in [(0, pt0), (1, pt1)]:
                    wtr = [("dv", f_ap[(hf, t)])]
                    if (hf, t - 1) in f_ev:
                        wtr.append(("ac", f_ev[(hf, t - 1)][half]))
                    elif hf == 1:
                        wtr.append(("ac", f_ev[(0, 7)][half]))
                    elif hf == 0 and t == 0:
                        wtr.append(("ac", P.n("ac")))
                    last = None
                    for j in range(8):
                        dc = half * 8 + j
                        last = P.emit("tensor", lambda te, pt=pt, j=j, dc=dc, vb=vb:
                                      te.transpose(out=pt[:, j * 128:(j + 1) * 128],
                                                   in_=vb[:, dc * 128:(dc + 1) * 128],
                                                   identity=c_id[:, :]),
                                      wait=wtr if j == 0 else (),
                                      inc="pe" if j == 7 else None)
                    f_tr.setdefault((hf, t), {})[half] = last
                ev = {}
                for half, pt in [(0, pt0), (1, pt1)]:
                    dst = XB[:, :].rearrange("p (dc q) -> p dc q", q=1024)[
                        :, half * 8:(half + 1) * 8, t * 128:(t + 1) * 128]
                    ev[half] = P.emit("scalar", lambda sc, pt=pt, dst=dst: sc.activation(
                        out=dst, in_=pt[:, :].rearrange("p (dc q) -> p dc q", q=128),
                        func=AF.Copy), wait=[("pe", f_tr[(hf, t)][half])], inc="ac")
                f_ev[(hf, t)] = ev

            for t in range(8):
                w = list(F_FENCE) if (hf == 0 and t < 2) else []
                if (hf, t - 2) in f_ap:
                    w += [("dv", f_ap[(hf, t - 2)]), ("ac", f_sq[(hf, t - 2)])]
                elif hf == 1 and t < 2:
                    w += [("dv", g_oasm[(0, 7, 3)])]
                    w += [("dv", f_ap[(0, 6 + t)]), ("ac", f_sq[(0, 6 + t)])]
                f_xld[(hf, t)] = P.dma("gpsimd", xft[t % 2][:, :],
                                       x1d[hf * 1024 + t * 128: hf * 1024 + (t + 1) * 128, :],
                                       wait=w)
                f_sq[(hf, t)], f_sqr[(hf, t)] = ln_stats(
                    xft[t % 2], t, 128, [f_xld[(hf, t)]], [f_xld[(hf, t)]])
                if t >= 1:
                    f_finish(t - 1)
            f_finish(7)
            hevs = {}
            for dc in range(DC):
                for lt2 in range(2):
                    k = dc * 2 + lt2
                    bank = pf[k % 2]
                    if k < 2:
                        w = [("ac", f_ev[(hf, 7)][0]), ("ac", f_ev[(hf, 7)][1]),
                             ("dv", CONST_DV)]
                    else:
                        w = [("ac", hevs[k - 2])]
                    mm = P.emit("tensor", lambda te, dc=dc, lt2=lt2, bank=bank: te.matmul(
                        out=bank[:, :], lhsT=c_hsu[:, dc * 128:(dc + 1) * 128],
                        rhs=XB[:, dc * 1024 + lt2 * 512: dc * 1024 + (lt2 + 1) * 512],
                        start=True, stop=True), wait=w, inc="pe")
                    hevs[k] = P.emit("scalar", lambda sc, dc=dc, lt2=lt2, bank=bank:
                                     sc.activation(
                                         out=XB[:, dc * 1024 + lt2 * 512: dc * 1024 + (lt2 + 1) * 512],
                                         in_=bank[:, :], func=AF.Copy),
                                     wait=[("pe", mm)], inc="ac")
            bstage = None
            for hp in [1, 2, 4, 8]:
                nb = DC // (2 * hp)
                vv = XB[:, :].rearrange("p (nb two hq) -> p nb two hq", two=2, hq=hp * 1024)
                a_, b_ = vv[:, :, 0, :], vv[:, :, 1, :]
                w = [("ac", hevs[31])] if hp == 1 else []
                P.emit("vector", lambda v, a_=a_, b_=b_: v.tensor_tensor(
                    out=a_, in0=a_, in1=b_, op=OP.add), wait=w)
                bstage = P.emit("vector", lambda v, a_=a_, b_=b_: v.scalar_tensor_tensor(
                    out=b_, in0=b_, scalar=-2.0, in1=a_, op0=OP.mult, op1=OP.add),
                    inc="dv")
            XB_READY[hf] = bstage
            if "xb" in taps and hf == 0:
                P.dma("sync", dbg["xb"], XB[:, :], wait=[("dv", bstage)])

            P.set_phase(f"ph_G{hf}")
            for ic in range(ICN):
                pr, sl = ic // 2, ic % 2
                if sl == 0:
                    def war(mmdict):
                        key = (hf, 2 * pr - 3)
                        if key in mmdict:
                            return [("pe", mmdict[key])]
                        if hf == 1:
                            return [("pe", mmdict[(0, ICN - 1)])]
                        return []
                    w = [("cc", CC_WUP)] + war(g_upmm)
                    if hf == 0 and pr == 0:
                        w += list(F_FENCE)
                    g_wld[(hf, pr, "u")] = P.dma(
                        "sync", v3(wupb[pr % 2], 256),
                        ag_wup[pr].rearrange("dc p q -> p dc q"), wait=w)
                    w2 = [("cc", CC_WGT)] + war(g_gtmm)
                    if hf == 0 and pr == 0:
                        w2 += list(F_FENCE)
                    g_wld[(hf, pr, "g")] = P.dma(
                        "sync", wgtb[pr % 2][:, :],
                        ag_wgt[pr].rearrange("dc p q -> p dc q"), wait=w2)
                for lt2 in range(2):
                    wmm = []
                    if lt2 == 0:
                        wmm = [("dm", g_wld[(hf, pr, "u")]), ("dv", XB_READY[hf])]
                    if (hf, ic - 1, lt2) in g_ystt:
                        wmm.append(("dv", g_ystt[(hf, ic - 1, lt2)]))
                    elif hf == 1:
                        wmm.append(("dv", g_ystt[(0, ICN - 1, lt2)]))
                    last = None
                    for dc in range(DC):
                        last = P.emit("tensor", lambda te, pr=pr, sl=sl, lt2=lt2, dc=dc:
                                      te.matmul(
                                          out=pf[lt2][:, :],
                                          lhsT=wupb[pr % 2][:, dc * 256 + sl * 128: dc * 256 + (sl + 1) * 128],
                                          rhs=XB[:, dc * 1024 + lt2 * 512: dc * 1024 + (lt2 + 1) * 512],
                                          start=(dc == 0), stop=(dc == DC - 1)),
                                      wait=wmm if dc == 0 else (),
                                      inc="pe" if dc == DC - 1 else None)
                g_upmm[(hf, ic)] = last
                for lt2 in range(2):
                    wmm = []
                    if lt2 == 0:
                        wmm = [("dm", g_wld[(hf, pr, "g")])]
                    if (hf, ic - 1, lt2) in g_gsev:
                        wmm.append(("ac", g_gsev[(hf, ic - 1, lt2)]))
                    elif hf == 1:
                        wmm.append(("ac", g_gsev[(0, ICN - 1, lt2)]))
                    last = None
                    for dc in range(DC):
                        last = P.emit("tensor", lambda te, pr=pr, sl=sl, lt2=lt2, dc=dc:
                                      te.matmul(
                                          out=pf[2 + lt2][:, :],
                                          lhsT=wgtb[pr % 2][:, dc * 256 + sl * 128: dc * 256 + (sl + 1) * 128],
                                          rhs=XB[:, dc * 1024 + lt2 * 512: dc * 1024 + (lt2 + 1) * 512],
                                          start=(dc == 0), stop=(dc == DC - 1)),
                                      wait=wmm if dc == 0 else (),
                                      inc="pe" if dc == DC - 1 else None)
                    g_gtmm[(hf, ic)] = last
                    wev = [("pe", last)]
                    if (hf, ic - 1, lt2) in g_ystt:
                        wev.append(("dv", g_ystt[(hf, ic - 1, lt2)]))
                    elif hf == 1:
                        wev.append(("dv", g_ystt[(0, ICN - 1, lt2)]))
                    g_gsev[(hf, ic, lt2)] = P.emit(
                        "scalar", lambda sc, lt2=lt2, ic=ic: sc.activation(
                            out=gsb[:, lt2 * 512:(lt2 + 1) * 512], in_=pf[2 + lt2][:, :],
                            func=AF.Silu, scale=c_wsg[:, ic:ic + 1]),
                        wait=wev, inc="ac")
                for lt2 in range(2):
                    wst = [("pe", g_upmm[(hf, ic)]), ("ac", g_gsev[(hf, ic, lt2)])]
                    gk = ic - 8
                    if gk >= 0:
                        wst.append(("pe", g_dnmm[(hf, gk // 8, 3, 7)]))
                    elif hf == 1:
                        wst.append(("pe", g_dnmm[(0, 7, 3, 7)]))
                    g_ystt[(hf, ic, lt2)] = P.emit(
                        "vector", lambda v, ic=ic, lt2=lt2: v.scalar_tensor_tensor(
                            out=YR[ic % 8][:, lt2 * 512:(lt2 + 1) * 512],
                            in0=pf[lt2][:, :], scalar=c_wsu[:, ic:ic + 1],
                            in1=gsb[:, lt2 * 512:(lt2 + 1) * 512],
                            op0=OP.mult, op1=OP.mult),
                        wait=wst, inc="dv")
                if "y0" in taps and hf == 0 and ic == 0:
                    P.dma("sync", dbg["y0"], YR[0][:, :],
                          wait=[("dv", g_ystt[(0, 0, 1)])])
                    pass
                if ic % 8 == 7:
                    gidx = ic // 8
                    for dtile in range(4):
                        w = [("cc", CC_WDN)]
                        if hf == 0 and gidx == 0 and dtile < 2:
                            w += list(F_FENCE)
                        if dtile >= 2:
                            w.append(("pe", g_dnmm[(hf, gidx, dtile - 2, 7)]))
                        elif gidx >= 1:
                            w.append(("pe", g_dnmm[(hf, gidx - 1, dtile + 2, 7)]))
                        elif hf == 1:
                            w.append(("pe", g_dnmm[(0, 7, dtile + 2, 7)]))
                        toks = []
                        for r in range(2):
                            src = ag_wdn.ap()[2 * dtile + r,
                                              gidx * 8:(gidx + 1) * 8, :, :] \
                                .rearrange("ic p q -> p ic q")
                            dst = wdng[dtile % 2][:, :].rearrange(
                                "p (ic q) -> p ic q", q=512)[:, :, r * 256:(r + 1) * 256]
                            toks.append(P.dma(
                                "gpsimd", dst, src, wait=w if r == 0 else ()))
                        g_dnld[(hf, gidx, dtile)] = toks
                        for lsub in range(8):
                            bank = pf[4 + (dtile * 8 + lsub) % 2]
                            wmm = list(g_dnld[(hf, gidx, dtile)]) + [
                                   ("dv", g_ystt[(hf, ic, 1)])]
                            fkey = (hf, gidx, dtile, lsub - 2)
                            fkey2 = (hf, gidx, dtile - 1, lsub + 6)
                            fkey3 = (hf, gidx - 1, 3, lsub + 6) if gidx >= 1 else \
                                ((0, 7, 3, lsub + 6) if hf == 1 else None)
                            if fkey in g_flush:
                                wmm.append(("dv", g_flush[fkey]))
                            elif fkey2 in g_flush:
                                wmm.append(("dv", g_flush[fkey2]))
                            elif fkey3 is not None and fkey3 in g_flush:
                                wmm.append(("dv", g_flush[fkey3]))
                            last = None
                            for k in range(8):
                                last = P.emit("tensor",
                                              lambda te, k=k, lsub=lsub, dtile=dtile, bank=bank:
                                              te.matmul(
                                                  out=bank[:, :],
                                                  lhsT=YR[k][:, lsub * 128:(lsub + 1) * 128],
                                                  rhs=wdng[dtile % 2][:, k * 512:(k + 1) * 512],
                                                  start=(k == 0), stop=(k == 7)),
                                              wait=wmm if k == 0 else (),
                                              inc="pe" if k == 7 else None)
                            g_dnmm[(hf, gidx, dtile, lsub)] = last
                            accsl = ACC[:, lsub * 2048 + dtile * 512:
                                        lsub * 2048 + (dtile + 1) * 512]
                            if gidx == 0:
                                g_flush[(hf, gidx, dtile, lsub)] = P.emit(
                                    "vector", lambda v, accsl=accsl, bank=bank:
                                    v.tensor_copy(out=accsl, in_=bank[:, :]),
                                    wait=[("pe", last)], inc="dv")
                            else:
                                g_flush[(hf, gidx, dtile, lsub)] = P.emit(
                                    "vector", lambda v, accsl=accsl, bank=bank:
                                    v.tensor_tensor(out=accsl, in0=accsl, in1=bank[:, :],
                                                    op=OP.add),
                                    wait=[("pe", last)], inc="dv")
            if "acc" in taps and hf == 0:
                P.dma("sync", dbg["acc"], ACC[:, :],
                      wait=[("dv", g_flush[(0, 7, 3, 7)])])
            P.set_phase(f"ph_out{hf}")
            for lsub in range(8):
                w = [("dv", g_flush[(hf, 7, 3, lsub)])]
                if lsub >= 2:
                    w.append(("dv", g_oasm[(hf, lsub - 2, 3)]))
                xl = P.dma("gpsimd", xft[lsub % 2][:, :],
                           x1d[hf * 1024 + lsub * 128: hf * 1024 + (lsub + 1) * 128, :],
                           wait=w + ([("ac", f_sq[(hf, 7)]), ("dv", f_ap[(hf, 7)])]
                                     if lsub < 2 else []))
                for piece in range(4):
                    w2 = [("dg", xl), ("dv", g_flush[(hf, 7, piece, lsub)])]
                    if (hf, lsub - 1, piece) in g_ost or lsub * 4 + piece >= 2:
                        prevk = (hf, lsub, piece - 2) if piece >= 2 else (hf, lsub - 1, piece + 2)
                        if prevk in g_ost:
                            w2.append(("dg", g_ost[prevk]))
                    g_oasm[(hf, lsub, piece)] = P.emit(
                        "vector", lambda v, lsub=lsub, piece=piece: v.tensor_tensor(
                            out=opb[piece % 2][:, :],
                            in0=ACC[:, lsub * 2048 + piece * 512: lsub * 2048 + (piece + 1) * 512],
                            in1=xft[lsub % 2][:, piece * 512:(piece + 1) * 512],
                            op=OP.add), wait=w2, inc="dv")
                    g_ost[(hf, lsub, piece)] = P.dma(
                        "gpsimd",
                        out[hf * 1024 + lsub * 128: hf * 1024 + (lsub + 1) * 128,
                            piece * 512:(piece + 1) * 512],
                        opb[piece % 2][:, :], wait=[("dv", g_oasm[(hf, lsub, piece)])])

        P.emit("gpsimd", lambda g: g.tensor_copy(out=sprt[:, 1:2], in_=sprt[:, 0:1]),
               wait=P.queue_barrier("gpsimd") + P.queue_barrier("sync") +
               [("cc", P.n("cc")), ("dv", P.n("dv")), ("ac", P.n("ac")),
                ("pe", P.n("pe"))],
               inc="pl")
        f_ctx.close()

        P.check()
        block = e(nc.Block())
        P.finalize(block)

    return nc



def _fm(vec, width):
    return np.ascontiguousarray(vec.reshape(width, 128).T)


def _prep_core(c, x2d, W):
    t0 = c * T
    rows = np.zeros((T + 2, D), np.float32)
    rows[1:T + 1] = x2d[t0:t0 + T]
    if c > 0:
        rows[0] = x2d[t0 - 1]
    if c < NCORES - 1:
        rows[T + 1] = x2d[t0 + T]

    wuT = W["wu"].T
    wu_u_t = np.ascontiguousarray(
        wuT.reshape(DC, 128, RC, 128).transpose(2, 0, 1, 3))
    wuv_t = np.ascontiguousarray(
        wuT.reshape(RC, 128, 4, 512).transpose(2, 0, 1, 3)).reshape(64, 128, 512)
    wvT = W["wv"].T
    wv_t = np.ascontiguousarray(wvT.reshape(DC, 128, RC, 128).transpose(2, 0, 1, 3))
    wg1_t = np.ascontiguousarray(
        W["wg1"].T.reshape(DC, 128, HCN, 128).transpose(2, 0, 1, 3))
    wg2_t = np.ascontiguousarray(
        W["wg2"].T.reshape(HCN, 128, RC, 128).transpose(2, 0, 1, 3))
    wupT = W["wup"].T
    wup_t = np.ascontiguousarray(
        wupT.reshape(DC, 128, 32, 256).transpose(2, 0, 1, 3))
    wgtT = W["wgt"].T
    wgt_t = np.ascontiguousarray(
        wgtT.reshape(DC, 128, 32, 256).transpose(2, 0, 1, 3))
    wdn_t = np.ascontiguousarray(
        W["wdn"][c * 256:(c + 1) * 256].reshape(2, 128, I_DIM))

    mask = np.zeros((128, NCORES), np.float32)
    mask[:, :c] = 1.0

    return {
        "xs": rows,
        "wu_u": wu_u_t,
        "wg1": wg1_t,
        "wg2": wg2_t,
        "wv_sl": np.ascontiguousarray(wv_t[2 * c: 2 * c + 2]),
        "wuv_sl": np.ascontiguousarray(wuv_t[8 * c: 8 * (c + 1)]),
        "wup_sl": np.ascontiguousarray(wup_t[4 * c: 4 * (c + 1)]),
        "wgt_sl": np.ascontiguousarray(wgt_t[4 * c: 4 * (c + 1)]),
        "wdn_sl": wdn_t,
        "bu": _fm(W["bu"], RC), "bv": _fm(W["bv"], RC),
        "bg1": _fm(W["bg1"], HCN), "bg2": _fm(W["bg2"], RC),
        "cw": np.ascontiguousarray(
            W["cw"].reshape(DC, 128, 3).transpose(1, 0, 2).reshape(128, DC * 3)),
        "cb": _fm(W["cb"], DC),
        "su": _fm(W["su"], DC),
        "sud": _fm(W["sud"], ICN),
        "wsu": _fm(W["wsu"], ICN),
        "wsg": _fm(W["wsg"], ICN),
        "wsd": np.ascontiguousarray(_fm(W["wsd"], DC)[:, 2 * c: 2 * c + 2]),
        "maskt": mask,
        "ident": np.eye(128, dtype=ml_dtypes.bfloat16),
        "h128": W["h128"],
    }


def _hadamard128():
    h = np.array([[1.0]], np.float32)
    while h.shape[0] < 128:
        h = np.block([[h, h], [h, -h]])
    return h.astype(ml_dtypes.bfloat16)


def kernel(L0_attn_x, L0_attn_Wu, L0_attn_bu, L0_attn_Wv, L0_attn_bv,
           L0_attn_Wg1, L0_attn_bg1, L0_attn_Wg2, L0_attn_bg2,
           L0_attn_conv_w, L0_attn_conv_b,
           L0_ffn_SU_ug, L0_ffn_SU_dp, L0_ffn_Wscale_ug, L0_ffn_Wscale_dp,
           L0_ffn_W_up, L0_ffn_W_gate, L0_ffn_W_down):
    x2d = np.ascontiguousarray(np.asarray(L0_attn_x, np.float32).reshape(NCORES * T, D))
    W = {
        "wu": np.asarray(L0_attn_Wu, np.float32),
        "wv": np.asarray(L0_attn_Wv, np.float32),
        "wg1": np.asarray(L0_attn_Wg1, np.float32),
        "wg2": np.asarray(L0_attn_Wg2, np.float32),
        "wup": np.asarray(L0_ffn_W_up, np.float32),
        "wgt": np.asarray(L0_ffn_W_gate, np.float32),
        "wdn": np.asarray(L0_ffn_W_down, np.float32),
        "bu": np.asarray(L0_attn_bu, np.float32),
        "bv": np.asarray(L0_attn_bv, np.float32),
        "bg1": np.asarray(L0_attn_bg1, np.float32),
        "bg2": np.asarray(L0_attn_bg2, np.float32),
        "cw": np.asarray(L0_attn_conv_w, np.float32),
        "cb": np.asarray(L0_attn_conv_b, np.float32),
        "su": np.asarray(L0_ffn_SU_ug, np.float32),
        "sud": np.asarray(L0_ffn_SU_dp, np.float32),
        "wsu": np.asarray(L0_ffn_Wscale_ug, np.float32)[:I_DIM],
        "wsg": np.asarray(L0_ffn_Wscale_ug, np.float32)[I_DIM:],
        "wsd": np.asarray(L0_ffn_Wscale_dp, np.float32),
        "h128": _hadamard128(),
    }
    in_maps = [_prep_core(c, x2d, W) for c in range(NCORES)]
    nc = build_program()
    res = run_bass_kernel_spmd(nc, in_maps, core_ids=list(range(NCORES)))
    outp = np.concatenate([res.results[c]["out"] for c in range(NCORES)], axis=0)
    if DEBUG:
        kernel.last_results = res.results
    return outp.reshape(1, NCORES * T, D)


# version 13 (final)
# speedup vs baseline: 1.0335x; 1.0335x over previous
import numpy as np
import ml_dtypes
import concourse.bass as bass
import concourse.mybir as mybir
from concourse.alu_op_type import AluOpType
from concourse.bass_utils import run_bass_kernel_spmd
from contextlib import ExitStack

F32 = mybir.dt.float32
BF16 = mybir.dt.bfloat16
AF = mybir.ActivationFunctionType
OP = AluOpType

NCORES = 8
T = 2048
D = 2048
DC = 16
RC = 16
HCN = 4
I_DIM = 8192
ICN = 64
NL = 16
XST = 2052
RSQ_D = float(1.0 / np.sqrt(2048.0))
RSQ_I = float(1.0 / np.sqrt(8192.0))
ENGS = ["sync", "tensor", "vector", "scalar", "gpsimd"]

DEBUG = False
DEBUG_TAPS = None


class Prog:

    NDMA = 6

    def __init__(self, ctx, nc):
        self.nc = nc
        self.items = {g: [] for g in ENGS}
        self.sems = {}
        self.cnt = {}
        names = ["pe", "dv", "ac", "pl", "cc"]
        self.pool = {"sync": [], "gpsimd": []}
        for q, pfx in [("sync", "dm"), ("gpsimd", "dg")]:
            for i in range(self.NDMA):
                names.append(f"{pfx}{i}")
                self.pool[q].append(f"{pfx}{i}")
        for name in names:
            self.sems[name] = ctx.enter_context(nc.semaphore("s_" + name))
            self.cnt[name] = 0
        self.qidx = {"sync": 0, "gpsimd": 0}
        self.cur_phase = "init"

    def emit(self, eng, fn, wait=(), inc=None, inck=1):
        norm = []
        for entry in wait:
            sn, v = entry
            if isinstance(v, tuple):
                sn, v = v
            norm.append((sn, v))
        wait = [(sn, v) for (sn, v) in norm if v > 0]
        if inc is not None:
            self.cnt[inc] += inck
        newv = self.cnt[inc] if inc is not None else None
        sems = self.sems

        def item(engobj, fn=fn, wait=wait, inc=inc, inck=inck):
            for sn, v in wait:
                engobj.wait_ge(sems[sn], v)
            instr = fn(engobj)
            if inc is not None:
                instr.then_inc(sems[inc], inck)
        self.items[eng].append((item, wait, inc, inck, self.cur_phase))
        return newv

    def dma(self, eng, out, in_, wait=()):
        q = eng if eng in self.pool else "gpsimd"
        sem = self.pool[q][self.qidx[q] % self.NDMA]
        self.qidx[q] += 1
        wait = [(sem, self.cnt[sem])] + list(wait)
        v = self.emit(eng, lambda e, out=out, in_=in_: e.dma_start(out=out, in_=in_),
                      wait=wait, inc=sem, inck=16)
        return (sem, v)

    def queue_barrier(self, q):
        return [(s, self.cnt[s]) for s in self.pool[q] if self.cnt[s] > 0]

    def n(self, name):
        return self.cnt[name]

    def check(self):
        vals = {k: 0 for k in self.sems}
        pos = {g: 0 for g in ENGS}
        progressed = True
        while progressed:
            progressed = False
            for g in ENGS:
                while pos[g] < len(self.items[g]):
                    _, wait, inc, inck, _ph = self.items[g][pos[g]]
                    if all(vals[sn] >= v for sn, v in wait):
                        if inc is not None:
                            vals[inc] += inck
                        pos[g] += 1
                        progressed = True
                    else:
                        break
        stuck = {g: (pos[g], len(self.items[g])) for g in ENGS if pos[g] < len(self.items[g])}
        if stuck:
            msgs = []
            for g, (p, n) in stuck.items():
                _, wait, _, _, _ph = self.items[g][p]
                msgs.append(f"{g}@{p}/{n} waits {wait} have "
                            f"{[(sn, vals[sn]) for sn, _ in wait]}")
            raise RuntimeError("build-time deadlock: " + "; ".join(msgs))

    def set_phase(self, name):
        self.cur_phase = name

    def finalize(self, block):
        nc = self.nc
        for g in ENGS:
            items = self.items[g]

            def body(engobj, items=items):
                cur = None
                sid = None
                for it in items:
                    ph = it[4]
                    if ph != cur:
                        if cur is not None:
                            nc.leave_named_scope(cur, sid, False)
                        sid, _ = nc.enter_named_scope(ph, False)
                        cur = ph
                    it[0](engobj)
                if cur is not None:
                    nc.leave_named_scope(cur, sid, False)
            getattr(block, g)(body)


def build_program():
    nc = bass.Bass("TRN2", target_bir_lowering=False)

    xs = nc.dram_tensor("xs", [T + 2, D], F32, kind="ExternalInput").ap()
    wu_u = nc.dram_tensor("wu_u", [RC, DC, 128, 128], F32, kind="ExternalInput").ap()
    wg1 = nc.dram_tensor("wg1", [HCN, DC, 128, 128], F32, kind="ExternalInput").ap()
    wg2 = nc.dram_tensor("wg2", [RC, HCN, 128, 128], F32, kind="ExternalInput").ap()
    wv_sl = nc.dram_tensor("wv_sl", [2, DC, 128, 128], F32, kind="ExternalInput").ap()
    wuv_sl = nc.dram_tensor("wuv_sl", [8, 128, 512], F32, kind="ExternalInput").ap()
    wup_sl = nc.dram_tensor("wup_sl", [4, DC, 128, 256], F32, kind="ExternalInput").ap()
    wgt_sl = nc.dram_tensor("wgt_sl", [4, DC, 128, 256], F32, kind="ExternalInput").ap()
    wdn_sl = nc.dram_tensor("wdn_sl", [2, 128, I_DIM], F32, kind="ExternalInput").ap()
    bu_i = nc.dram_tensor("bu", [128, RC], F32, kind="ExternalInput").ap()
    bv_i = nc.dram_tensor("bv", [128, RC], F32, kind="ExternalInput").ap()
    bg1_i = nc.dram_tensor("bg1", [128, HCN], F32, kind="ExternalInput").ap()
    bg2_i = nc.dram_tensor("bg2", [128, RC], F32, kind="ExternalInput").ap()
    cw_i = nc.dram_tensor("cw", [128, DC * 3], F32, kind="ExternalInput").ap()
    cb_i = nc.dram_tensor("cb", [128, DC], F32, kind="ExternalInput").ap()
    su_i = nc.dram_tensor("su", [128, DC], F32, kind="ExternalInput").ap()
    sud_i = nc.dram_tensor("sud", [128, ICN], F32, kind="ExternalInput").ap()
    wsu_i = nc.dram_tensor("wsu", [128, ICN], F32, kind="ExternalInput").ap()
    wsg_i = nc.dram_tensor("wsg", [128, ICN], F32, kind="ExternalInput").ap()
    wsd_i = nc.dram_tensor("wsd", [128, 2], F32, kind="ExternalInput").ap()
    mask_i = nc.dram_tensor("maskt", [128, NCORES], F32, kind="ExternalInput").ap()
    id_i = nc.dram_tensor("ident", [128, 128], BF16, kind="ExternalInput").ap()
    h128_i = nc.dram_tensor("h128", [128, 128], BF16, kind="ExternalInput").ap()

    out = nc.dram_tensor("out", [T, D], F32, kind="ExternalOutput").ap()

    ag_wv_in = nc.dram_tensor("ag_wv_in", [2, DC, 128, 128], BF16)
    ag_wv = nc.dram_tensor("ag_wv", [RC, DC, 128, 128], BF16, addr_space="Shared")
    ag_wuv_in = nc.dram_tensor("ag_wuv_in", [8, 128, 512], BF16)
    ag_wuv = nc.dram_tensor("ag_wuv", [64, 128, 512], BF16, addr_space="Shared")
    ag_wup_in = nc.dram_tensor("ag_wup_in", [4, DC, 128, 256], BF16)
    ag_wup = nc.dram_tensor("ag_wup", [32, DC, 128, 256], BF16, addr_space="Shared")
    ag_wgt_in = nc.dram_tensor("ag_wgt_in", [4, DC, 128, 256], BF16)
    ag_wgt = nc.dram_tensor("ag_wgt", [32, DC, 128, 256], BF16, addr_space="Shared")
    ag_wdn_in = nc.dram_tensor("ag_wdn_in", [ICN, 128, 256], BF16)
    ag_wdn = nc.dram_tensor("ag_wdn", [NCORES, ICN, 128, 256], BF16, addr_space="Shared")
    ag_s_in = nc.dram_tensor("ag_s_in", [1, T], F32)
    ag_s = nc.dram_tensor("ag_s", [NCORES, T], F32, addr_space="Shared")
    x1d = nc.dram_tensor("x1d", [T, D], F32)
    wdn_b = nc.dram_tensor("wdn_b", [2, 128, I_DIM], F32)

    taps = (set() if not DEBUG else
            ({"xn", "hb", "kc", "soff", "go", "x1", "wdn", "xb", "acc", "y0", "gs0"}
             if DEBUG_TAPS is None else set(DEBUG_TAPS)))
    dbg = {}
    if True:
        for name, shape, dt in [
            ("xn", [128, DC * XST], BF16), ("hb", [128, HCN * T], BF16),
            ("kc", [128, RC * T], BF16), ("go", [128, RC * T], BF16),
            ("soff", [128, NCORES * RC + RC], F32), ("x1", [T, D], F32),
            ("wdn", [ICN, 128, 256], BF16), ("xb", [128, DC * 1024], BF16),
            ("acc", [128, 8 * 2048], F32), ("y0", [128, 1024], BF16),
        ]:
            if name in taps:
                dbg[name] = nc.dram_tensor("dbg_" + name, shape, dt, kind="ExternalOutput").ap()

    octx = ExitStack()
    with octx:
        e = octx.enter_context
        P = Prog(octx, nc)

        c_bu = e(nc.sbuf_tensor("c_bu", [128, RC], F32))
        c_bv = e(nc.sbuf_tensor("c_bv", [128, RC], F32))
        c_bg1 = e(nc.sbuf_tensor("c_bg1", [128, HCN], F32))
        c_bg2 = e(nc.sbuf_tensor("c_bg2", [128, RC], F32))
        c_cw = e(nc.sbuf_tensor("c_cw", [128, DC * 3], F32))
        c_cb = e(nc.sbuf_tensor("c_cb", [128, DC], F32))
        c_su = e(nc.sbuf_tensor("c_su", [128, DC], F32))
        c_sud = e(nc.sbuf_tensor("c_sud", [128, ICN], F32))
        c_wsu = e(nc.sbuf_tensor("c_wsu", [128, ICN], F32))
        c_wsg = e(nc.sbuf_tensor("c_wsg", [128, ICN], F32))
        c_wsd = e(nc.sbuf_tensor("c_wsd", [128, 2], F32))
        c_mask = e(nc.sbuf_tensor("c_mask", [128, NCORES], F32))
        c_id = e(nc.sbuf_tensor("c_id", [128, 128], BF16))
        c_h128 = e(nc.sbuf_tensor("c_h128", [128, 128], BF16))
        c_hsu = e(nc.sbuf_tensor("c_hsu", [128, DC * 128], BF16))
        c_ones = e(nc.sbuf_tensor("c_ones", [128, T], BF16))
        st_s1 = e(nc.sbuf_tensor("st_s1", [128, NL], F32))
        st_s2 = e(nc.sbuf_tensor("st_s2", [128, NL], F32))
        st_m = e(nc.sbuf_tensor("st_m", [128, NL], F32))
        st_r = e(nc.sbuf_tensor("st_r", [128, NL], F32))
        st_t = e(nc.sbuf_tensor("st_t", [128, NL], F32))
        st_n = e(nc.sbuf_tensor("st_n", [128, NL], F32))
        c_eps = e(nc.sbuf_tensor("c_eps", [128, 1], F32))
        sfm = e(nc.sbuf_tensor("sfm", [128, RC], F32))
        sprt = e(nc.sbuf_tensor("sprt", [128, 4], F32))
        sall = e(nc.sbuf_tensor("sall", [128, NCORES * RC], F32))
        soff = e(nc.sbuf_tensor("soff", [128, RC], F32))
        GB = e(nc.sbuf_tensor("GB", [128, T], BF16))
        VB = e(nc.sbuf_tensor("VB", [128, T], BF16))

        pf = [e(nc.psum_tensor(f"pf{i}", [128, 512], F32)) for i in range(6)]
        pt0 = e(nc.psum_tensor("pt0", [128, 1024], BF16))
        pt1 = e(nc.psum_tensor("pt1", [128, 1024], BF16))

        attn_ctx = ExitStack()
        XN = attn_ctx.enter_context(nc.sbuf_tensor("XN", [128, DC * XST], BF16))
        KC = attn_ctx.enter_context(nc.sbuf_tensor("KC", [128, RC * T], BF16))
        HB = attn_ctx.enter_context(nc.sbuf_tensor("HB", [128, HCN * T], BF16))

        p0_ctx = ExitStack()
        p0f = p0_ctx.enter_context(nc.sbuf_tensor("p0f", [128, 1024], F32))
        p0b = p0_ctx.enter_context(nc.sbuf_tensor("p0b", [128, 1024], BF16))

        a_ctx = ExitStack()
        XT0 = a_ctx.enter_context(nc.sbuf_tensor("XT0", [128, D], F32))
        XT1 = a_ctx.enter_context(nc.sbuf_tensor("XT1", [128, D], F32))
        XT2 = a_ctx.enter_context(nc.sbuf_tensor("XT2", [128, D], F32))
        VBa = a_ctx.enter_context(nc.sbuf_tensor("VBa", [128, T], BF16))

        def xn_tok(dc, a, w):
            return XN[:, dc * XST + 2 + a: dc * XST + 2 + a + w]

        def v3(tile, q, width=None):
            ap = tile[:, :] if width is None else tile[:, :width]
            return ap.rearrange("p (a q) -> p a q", q=q)

        P.set_phase("ph_const")
        CONST_DM = 0
        for dst, src in [(c_bu, bu_i), (c_bv, bv_i), (c_bg1, bg1_i), (c_bg2, bg2_i),
                         (c_cw, cw_i), (c_cb, cb_i), (c_su, su_i), (c_sud, sud_i),
                         (c_wsu, wsu_i), (c_wsg, wsg_i), (c_wsd, wsd_i), (c_mask, mask_i),
                         (c_id, id_i), (c_h128, h128_i)]:
            P.dma("sync", dst[:, :], src[:, :])
        CONST_BARRIER = P.queue_barrier("sync")
        P.emit("vector", lambda v: v.memset(c_ones[:, :], 1.0))
        P.emit("vector", lambda v: v.memset(c_eps[:, :], 1e-5))
        P.emit("vector", lambda v: v.tensor_scalar(out=c_su[:, :], in0=c_su[:, :],
                                                   scalar1=RSQ_D, scalar2=None, op0=OP.mult),
               wait=CONST_BARRIER)
        SU_DV = P.emit("vector", lambda v: v.tensor_scalar(out=c_wsd[:, :], in0=c_wsd[:, :],
                                                           scalar1=RSQ_I, scalar2=None, op0=OP.mult),
                       inc="dv")
        for dc in range(DC):
            P.emit("vector", lambda v, dc=dc: v.tensor_scalar(
                out=c_hsu[:, dc * 128:(dc + 1) * 128], in0=c_h128[:, :],
                scalar1=c_su[:, dc:dc + 1], scalar2=None, op0=OP.mult),
                wait=[("dv", SU_DV)] if dc == 0 else ())
        CONST_DV = P.emit("vector", lambda v: v.tensor_copy(out=sprt[:, 0:1], in_=c_su[:, 0:1]),
                          inc="dv")

        P.set_phase("ph_p0cast")
        def ag(kind_in, kind_out):
            def fn(g):
                return g.collective_compute("AllGather", OP.bypass,
                                            replica_groups=[list(range(NCORES))],
                                            ins=[kind_in.ap().opt()],
                                            outs=[kind_out.ap().opt()])
            return fn

        p0_state = {"cast": 0, "st": None}

        def p0_cast(src_view, dst_view, q, queue="gpsimd"):
            ld = P.dma(queue, v3(p0f, q), src_view,
                       wait=[("ac", p0_state["cast"])])
            wcast = [ld]
            if p0_state["st"] is not None:
                wcast.append(p0_state["st"])
            cast = P.emit("scalar", lambda sc: sc.activation(out=p0b[:, :], in_=p0f[:, :],
                                                             func=AF.Copy),
                          wait=wcast, inc="ac")
            p0_state["cast"] = cast
            st = P.dma(queue, dst_view, v3(p0b, q), wait=[("ac", cast)])
            p0_state["st"] = st
            return st

        for j in range(2):
            for hh in range(2):
                st = p0_cast(wv_sl[j, hh * 8:(hh + 1) * 8].rearrange("dc p q -> p dc q"),
                             ag_wv_in[j, hh * 8:(hh + 1) * 8].rearrange("dc p q -> p dc q"), 128)
        CC_WV = P.emit("gpsimd", ag(ag_wv_in, ag_wv), wait=P.queue_barrier("gpsimd"), inc="cc")
        for j in range(4):
            st = p0_cast(wuv_sl[j * 2:(j + 1) * 2].rearrange("b p q -> p b q"),
                         ag_wuv_in[j * 2:(j + 1) * 2].rearrange("b p q -> p b q"), 512)
        CC_WUV = P.emit("gpsimd", ag(ag_wuv_in, ag_wuv), wait=P.queue_barrier("gpsimd"), inc="cc")
        P0_AC = P.n("ac")

        P.set_phase("ph_A")
        def ln_stats(xt, col, psrc, w_red, w_sq):
            red = P.emit("vector", lambda v: v.tensor_reduce(
                out=st_s1[:psrc, col:col + 1], in_=xt[:psrc, :],
                axis=mybir.AxisListType.X, op=OP.add), wait=w_red, inc="dv")
            P.emit("vector", lambda v: v.tensor_scalar(
                out=st_m[:psrc, col:col + 1], in0=st_s1[:psrc, col:col + 1],
                scalar1=1.0 / D, scalar2=None, op0=OP.mult), wait=[("dv", red)])
            negm = P.emit("vector", lambda v: v.tensor_scalar(
                out=st_n[:psrc, col:col + 1], in0=st_s1[:psrc, col:col + 1],
                scalar1=-1.0 / D, scalar2=None, op0=OP.mult), inc="dv")
            sq = P.emit("scalar", lambda sc: sc.activation(
                out=GB[:psrc, :], in_=xt[:psrc, :], func=AF.Square,
                bias=st_n[:psrc, col:col + 1],
                accum_out=st_s2[:psrc, col:col + 1]),
                wait=w_sq + [("dv", negm)], inc="ac")
            sqr = P.emit("scalar", lambda sc: sc.activation(
                out=st_t[:psrc, col:col + 1], in_=st_s2[:psrc, col:col + 1],
                func=AF.Sqrt, scale=1.0 / D, bias=c_eps[:psrc, 0:1]),
                wait=[("ac", sq)], inc="ac")
            return sq, sqr

        def ln_finish(xt, col, dst, psrc, sqr, w_ap):
            recip = P.emit("vector", lambda v: v.reciprocal(
                out=st_r[:psrc, col:col + 1], in_=st_t[:psrc, col:col + 1]),
                wait=[("ac", sqr)], inc="dv")
            apply = P.emit("vector", lambda v: v.tensor_scalar(
                out=dst[:psrc, :], in0=xt[:psrc, :],
                scalar1=st_m[:psrc, col:col + 1], scalar2=st_r[:psrc, col:col + 1],
                op0=OP.subtract, op1=OP.mult),
                wait=w_ap + [("dv", recip)], inc="dv")
            return apply

        def ln_tile(xt, col, dst, psrc, kind):
            red = P.emit("vector", lambda v: v.tensor_reduce(
                out=st_s1[:psrc, col:col + 1], in_=xt[:psrc, :],
                axis=mybir.AxisListType.X, op=OP.add), wait=kind["w_red"], inc="dv")
            P.emit("vector", lambda v: v.tensor_scalar(
                out=st_m[:psrc, col:col + 1], in0=st_s1[:psrc, col:col + 1],
                scalar1=1.0 / D, scalar2=None, op0=OP.mult), wait=[("dv", red)])
            negm = P.emit("vector", lambda v: v.tensor_scalar(
                out=st_n[:psrc, col:col + 1], in0=st_s1[:psrc, col:col + 1],
                scalar1=-1.0 / D, scalar2=None, op0=OP.mult), inc="dv")
            sq = P.emit("scalar", lambda sc: sc.activation(
                out=GB[:psrc, :], in_=xt[:psrc, :], func=AF.Square,
                bias=st_n[:psrc, col:col + 1],
                accum_out=st_s2[:psrc, col:col + 1]),
                wait=kind["w_sq"] + [("dv", negm)], inc="ac")
            sqr = P.emit("scalar", lambda sc: sc.activation(
                out=st_t[:psrc, col:col + 1], in_=st_s2[:psrc, col:col + 1],
                func=AF.Sqrt, scale=1.0 / D, bias=c_eps[:psrc, 0:1]),
                wait=[("ac", sq)], inc="ac")
            recip = P.emit("vector", lambda v: v.reciprocal(
                out=st_r[:psrc, col:col + 1], in_=st_t[:psrc, col:col + 1]),
                wait=[("ac", sqr)], inc="dv")
            apply = P.emit("vector", lambda v: v.tensor_scalar(
                out=dst[:psrc, :], in0=xt[:psrc, :],
                scalar1=st_m[:psrc, col:col + 1], scalar2=st_r[:psrc, col:col + 1],
                op0=OP.subtract, op1=OP.mult),
                wait=kind["w_ap"] + [("dv", recip)], inc="dv")
            return sq, apply

        a_dma = {}
        a_sq = {}
        a_sqr = {}
        a_apply = {}
        a_tr = {}
        a_ev = {}
        XTs = [XT0, XT1, XT2]
        VBs = [VB, VBa]

        def a_finish(t):
            xt = XTs[t % 3]
            vb = VBs[t % 2]
            w_ap = [("pe", a_tr[t - 2][1])] if t >= 2 else []
            a_apply[t] = ln_finish(xt, t, vb, 128, a_sqr[t], w_ap)
            for half, pt in [(0, pt0), (1, pt1)]:
                wtr = [("dv", a_apply[t])]
                if t >= 1:
                    wtr.append(("ac", a_ev[t - 1][half]))
                last = None
                for j in range(8):
                    dc = half * 8 + j
                    last = P.emit("tensor", lambda te, pt=pt, j=j, dc=dc, vb=vb:
                                  te.transpose(
                                      out=pt[:, j * 128:(j + 1) * 128],
                                      in_=vb[:, dc * 128:(dc + 1) * 128],
                                      identity=c_id[:, :]),
                                  wait=wtr if j == 0 else (),
                                  inc="pe" if j == 7 else None)
                a_tr.setdefault(t, {})[half] = last
            for half, pt in [(0, pt0), (1, pt1)]:
                dst = XN[:, :].rearrange("p (dc q) -> p dc q", q=XST)[
                    :, half * 8:(half + 1) * 8, 2 + t * 128: 2 + (t + 1) * 128]
                a_ev.setdefault(t, {})[half] = P.emit(
                    "scalar", lambda sc, pt=pt, dst=dst: sc.activation(
                        out=dst, in_=pt[:, :].rearrange("p (dc q) -> p dc q", q=128),
                        func=AF.Copy),
                    wait=[("pe", a_tr[t][half])], inc="ac")

        for t in range(NL):
            xt = XTs[t % 3]
            w = list(CONST_BARRIER) if t == 0 else []
            if t >= 3:
                w = [("dv", a_apply[t - 3]), ("ac", a_sq[t - 3])]
            a_dma[t] = P.dma("sync", xt[:, :], xs[1 + t * 128: 1 + (t + 1) * 128, :], wait=w)
            a_sq[t], a_sqr[t] = ln_stats(xt, t, 128, [a_dma[t]], [a_dma[t]])
            if t >= 1:
                a_finish(t - 1)
        a_finish(NL - 1)
        hd0 = P.dma("sync", XT0[0:1, :], xs[0:1, :],
                    wait=[("dv", a_apply[15]), ("ac", a_sq[15])])
        hd1 = P.dma("sync", XT0[1:2, :], xs[T + 1:T + 2, :])
        kind = {"w_sq": [hd0, hd1], "w_red": [hd0, hd1],
                "w_ap": [("pe", a_tr[15][1])]}
        h_sq, h_apply = ln_tile(XT0, 0, VB, 2, kind)
        last = None
        for dc in range(DC):
            last = P.emit("tensor", lambda te, dc=dc: te.transpose(
                out=pt0[:, dc * 2:(dc + 1) * 2], in_=VB[0:2, dc * 128:(dc + 1) * 128],
                identity=c_id[0:2, 0:2]),
                wait=[("dv", h_apply), ("ac", a_ev[15][0])] if dc == 0 else (),
                inc="pe" if dc == DC - 1 else None)
        halo_tr = last
        xnv = XN[:, :].rearrange("p (dc q) -> p dc q", q=XST)
        ptv = pt0[:, 0:32].rearrange("p (dc q) -> p dc q", q=2)
        P.emit("scalar", lambda sc: sc.activation(
            out=xnv[:, :, 1:2], in_=ptv[:, :, 0:1], func=AF.Copy),
            wait=[("pe", halo_tr)], inc="ac")
        A_AC_END = P.emit("scalar", lambda sc: sc.activation(
            out=xnv[:, :, 2050:2051], in_=ptv[:, :, 1:2], func=AF.Copy), inc="ac")
        A_DV_END = h_apply
        A_PE_END = halo_tr

        if "xn" in taps:
            P.dma("sync", dbg["xn"], XN[:, :], wait=[("ac", A_AC_END)])

        a_ctx.close()


        wdn_ctx = ExitStack()
        wd1 = wdn_ctx.enter_context(nc.sbuf_tensor("wd1", [128, 1024], F32))
        wd2 = wdn_ctx.enter_context(nc.sbuf_tensor("wd2", [128, 1024], F32))
        wdt = wdn_ctx.enter_context(nc.sbuf_tensor("wdt", [128, 512], F32))
        wdc = wdn_ctx.enter_context(nc.sbuf_tensor("wdc", [128, 1024], BF16))
        wds = wdn_ctx.enter_context(nc.sbuf_tensor("wds", [128, 1024], BF16))
        wd3 = p0f
        strm_ctx = ExitStack()
        GBb = strm_ctx.enter_context(nc.sbuf_tensor("GBb", [128, T], BF16))
        wfp = strm_ctx.enter_context(nc.sbuf_tensor("wfp", [128, D], F32))
        wb0 = strm_ctx.enter_context(nc.sbuf_tensor("wb0", [128, D], BF16))
        wb1 = strm_ctx.enter_context(nc.sbuf_tensor("wb1", [128, D], BF16))
        wg2f = strm_ctx.enter_context(nc.sbuf_tensor("wg2f", [128, 512], F32))
        wg2b0 = strm_ctx.enter_context(nc.sbuf_tensor("wg2b0", [128, 512], BF16))
        wg2b1 = strm_ctx.enter_context(nc.sbuf_tensor("wg2b1", [128, 512], BF16))
        A_FENCE = [("dv", A_DV_END), ("ac", A_AC_END), ("pe", A_PE_END)]

        P.set_phase("ph_h")
        wb = [wb0, wb1]
        h_cast = {}
        h_mm = {}
        h_ev = {}
        for hc in range(HCN):
            w = list(A_FENCE) if hc == 0 else [("dv", h_cast[hc - 1])]
            ld = P.dma("sync", v3(wfp, 128), wg1[hc].rearrange("dc p q -> p dc q"), wait=w)
            wcast = [("dm", ld)]
            if hc >= 2:
                wcast.append(("pe", h_mm[hc - 2][3]))
            h_cast[hc] = P.emit("vector", lambda v, hc=hc: v.tensor_scalar(
                out=wb[hc % 2][:, :], in0=wfp[:, :], scalar1=1.0, scalar2=None, op0=OP.mult),
                wait=wcast, inc="dv")
            for lt in range(4):
                wmm = [("dv", h_cast[hc])] if lt == 0 else []
                if hc >= 1:
                    wmm.append(("ac", h_ev[hc - 1][lt]))
                last = None
                for dc in range(DC):
                    last = P.emit("tensor", lambda te, hc=hc, lt=lt, dc=dc: te.matmul(
                        out=pf[lt][:, :], lhsT=wb[hc % 2][:, dc * 128:(dc + 1) * 128],
                        rhs=xn_tok(dc, lt * 512, 512),
                        start=(dc == 0), stop=(dc == DC - 1)),
                        wait=wmm if dc == 0 else (), inc="pe" if dc == DC - 1 else None)
                h_mm.setdefault(hc, {})[lt] = last
                h_ev.setdefault(hc, {})[lt] = P.emit("scalar", lambda sc, hc=hc, lt=lt:
                    sc.activation(out=HB[:, hc * T + lt * 512: hc * T + (lt + 1) * 512],
                                  in_=pf[lt][:, :], func=AF.Relu,
                                  bias=c_bg1[:, hc:hc + 1]),
                    wait=[("pe", h_mm[hc][lt])], inc="ac")
        H_DONE_AC = h_ev[HCN - 1][3]
        if "hb" in taps:
            P.dma("sync", dbg["hb"], HB[:, :], wait=[("ac", H_DONE_AC)])

        P.set_phase("ph_B")
        wg2b = [wg2b0, wg2b1]

        def g_block(rc, phase, prev_gmm, prev_cast, gb_war, gev_war, gb):
            w = [("dv", prev_cast)] if prev_cast else list(A_FENCE)
            ld = P.dma("sync", v3(wg2f, 128), wg2[rc].rearrange("hc p q -> p hc q"), wait=w)
            wcast = [("dm", ld)]
            if prev_gmm is not None:
                wcast.append(("pe", prev_gmm[3]))
            cast = P.emit("vector", lambda v, rc=rc: v.tensor_scalar(
                out=wg2b[rc % 2][:, :], in0=wg2f[:, :], scalar1=1.0, scalar2=None,
                op0=OP.mult), wait=wcast, inc="dv")
            gmm = {}
            gev = {}
            for lt in range(4):
                bank = pf[4 + lt % 2]
                wmm = [("dv", cast)] if lt == 0 else []
                if lt == 0 and phase == "B" and rc == 0:
                    wmm.append(("ac", H_DONE_AC))
                if lt >= 2:
                    wmm.append(("ac", gev[lt - 2]))
                elif gev_war is not None:
                    wmm.append(("ac", gev_war[lt + 2]))
                last = None
                for hc in range(HCN):
                    last = P.emit("tensor", lambda te, rc=rc, lt=lt, hc=hc, bank=bank:
                                  te.matmul(out=bank[:, :],
                                            lhsT=wg2b[rc % 2][:, hc * 128:(hc + 1) * 128],
                                            rhs=HB[:, hc * T + lt * 512: hc * T + (lt + 1) * 512],
                                            start=(hc == 0), stop=(hc == HCN - 1)),
                                  wait=wmm if hc == 0 else (),
                                  inc="pe" if hc == HCN - 1 else None)
                gmm[lt] = last
                wev = [("pe", gmm[lt])]
                if gb_war is not None and lt == 0:
                    wev.append(("dv", gb_war))
                gev[lt] = P.emit("scalar", lambda sc, rc=rc, lt=lt, bank=bank, gb=gb:
                                 sc.activation(out=gb[:, lt * 512:(lt + 1) * 512],
                                               in_=bank[:, :], func=AF.Sigmoid,
                                               bias=c_bg2[:, rc:rc + 1]),
                                 wait=wev, inc="ac")
            return cast, gmm, gev

        b_cast = {}
        b_gmm = {}
        b_gev = {}
        b_wu_cast = {}
        b_umm = {}
        b_ugev = {}
        b_scan = {}
        for rc in range(RC):
            gb = GB if rc % 2 == 0 else GBb
            b_cast[rc], b_gmm[rc], b_gev[rc] = g_block(
                rc, "B",
                prev_gmm=b_gmm.get(rc - 1),
                prev_cast=b_cast.get(rc - 1),
                gb_war=b_scan.get(rc - 2),
                gev_war=b_gev.get(rc - 1), gb=gb)
            w = [("dv", b_wu_cast[rc - 1])] if rc > 0 else \
                list(A_FENCE) + [("dv", h_cast[HCN - 1])]
            ld = P.dma("sync", v3(wfp, 128), wu_u[rc].rearrange("dc p q -> p dc q"), wait=w)
            wcast = [("dm", ld)]
            if rc >= 2:
                wcast.append(("pe", b_umm[rc - 2][3]))
            else:
                wcast.append(("pe", h_mm[HCN - 1][3]))
            b_wu_cast[rc] = P.emit("vector", lambda v, rc=rc: v.tensor_scalar(
                out=wb[rc % 2][:, :], in0=wfp[:, :], scalar1=1.0, scalar2=None, op0=OP.mult),
                wait=wcast, inc="dv")
            for lt in range(4):
                wmm = [("dv", b_wu_cast[rc])] if lt == 0 else []
                if rc >= 1:
                    wmm.append(("dv", b_ugev[rc - 1][lt]))
                else:
                    wmm.append(("ac", h_ev[HCN - 1][lt]))
                last = None
                for dc in range(DC):
                    last = P.emit("tensor", lambda te, rc=rc, lt=lt, dc=dc: te.matmul(
                        out=pf[lt][:, :], lhsT=wb[rc % 2][:, dc * 128:(dc + 1) * 128],
                        rhs=xn_tok(dc, lt * 512, 512),
                        start=(dc == 0), stop=(dc == DC - 1)),
                        wait=wmm if dc == 0 else (), inc="pe" if dc == DC - 1 else None)
                b_umm.setdefault(rc, {})[lt] = last
                b_ugev.setdefault(rc, {})[lt] = P.emit(
                    "vector", lambda v, rc=rc, lt=lt, gb=gb: v.scalar_tensor_tensor(
                        out=gb[:, lt * 512:(lt + 1) * 512], in0=pf[lt][:, :],
                        scalar=c_bu[:, rc:rc + 1],
                        in1=gb[:, lt * 512:(lt + 1) * 512],
                        op0=OP.add, op1=OP.mult),
                    wait=[("pe", b_umm[rc][lt]), ("ac", b_gev[rc][lt])], inc="dv")

            for slot in range(2):
                pidx = rc * 2 + slot
                wsl, agin = (wup_sl, ag_wup_in) if pidx < 16 else (wgt_sl, ag_wgt_in)
                pj = pidx % 16
                p0_cast(wsl[pj // 4, (pj % 4) * 4:(pj % 4) * 4 + 4]
                        .rearrange("dc p q -> p dc q"),
                        agin[pj // 4, (pj % 4) * 4:(pj % 4) * 4 + 4]
                        .rearrange("dc p q -> p dc q"), 256, queue="sync")
            scn = P.emit("vector", lambda v, rc=rc, gb=gb: v.tensor_tensor_scan(
                out=KC[:, rc * T:(rc + 1) * T], data0=c_ones[:, :], data1=gb[:, :],
                initial=0.0, op0=OP.mult, op1=OP.add), inc="dv")
            b_scan[rc] = P.emit("vector", lambda v, rc=rc: v.tensor_copy(
                out=sfm[:, rc:rc + 1], in_=KC[:, rc * T + T - 1: rc * T + T]),
                wait=[("dv", scn)], inc="dv")
        B_SCAN_END = b_scan[RC - 1]
        if "kc" in taps:
            P.dma("sync", dbg["kc"], KC[:, :], wait=[("dv", B_SCAN_END)])

        P.set_phase("ph_wdn")
        wdn_state = {"tail": None, "cast": None, "ptwar": None}

        def wdn_butterflies(dcn):
            st_prev = None
            for strip in range(8):
                w = []
                if dcn == 0 and strip == 0:
                    w = list(A_FENCE)
                if st_prev is not None:
                    w.append(st_prev)
                ld = P.dma("gpsimd", wd1[:, :],
                           wdn_sl[dcn][:, strip * 1024:(strip + 1) * 1024], wait=w)
                first = True
                for hs in range(10):
                    h = 1 << hs
                    nb = 1024 // (2 * h)
                    vv = wd1[:, :].rearrange("p (nb two h) -> p nb two h", two=2, h=h)
                    a_, b_ = vv[:, :, 0, :], vv[:, :, 1, :]
                    tmp = wdt[:, :].rearrange("p (nb h) -> p nb h", h=h)[:, 0:nb, :]
                    P.emit("gpsimd", lambda g, a_=a_, b_=b_, tmp=tmp: g.tensor_tensor(
                        out=tmp, in0=a_, in1=b_, op=OP.subtract),
                        wait=[ld] if first else ())
                    first = False
                    P.emit("gpsimd", lambda g, a_=a_, b_=b_: g.tensor_tensor(
                        out=a_, in0=a_, in1=b_, op=OP.add))
                    P.emit("gpsimd", lambda g, b_=b_, tmp=tmp: g.tensor_copy(
                        out=b_, in_=tmp))
                st_prev = P.dma("gpsimd", wdn_b[dcn][:, strip * 1024:(strip + 1) * 1024],
                                wd1[:, :])
            pairs = {1024: [(0, 1), (2, 3), (4, 5), (6, 7)],
                     2048: [(0, 2), (1, 3), (4, 6), (5, 7)],
                     4096: [(0, 4), (1, 5), (2, 6), (3, 7)]}
            st_a = [st_prev]
            for h in [1024, 2048, 4096]:
                for (ia, ib) in pairs[h]:
                    lda = P.dma("gpsimd", wd1[:, :], wdn_b[dcn][:, ia * 1024:(ia + 1) * 1024],
                                wait=list(st_a))
                    ldb = P.dma("gpsimd", wd2[:, :], wdn_b[dcn][:, ib * 1024:(ib + 1) * 1024])
                    pl_pair = None
                    for hh2 in range(2):
                        av = wd1[:, hh2 * 512:(hh2 + 1) * 512]
                        bv = wd2[:, hh2 * 512:(hh2 + 1) * 512]
                        P.emit("gpsimd", lambda g, av=av, bv=bv: g.tensor_tensor(
                            out=wdt[:, :], in0=av, in1=bv, op=OP.subtract),
                            wait=[lda, ldb] if hh2 == 0 else ())
                        P.emit("gpsimd", lambda g, av=av, bv=bv: g.tensor_tensor(
                            out=av, in0=av, in1=bv, op=OP.add))
                        pl_pair = P.emit("gpsimd", lambda g, bv=bv: g.tensor_copy(
                            out=bv, in_=wdt[:, :]), inc="pl" if hh2 == 1 else None)
                    sta = P.dma("gpsimd", wdn_b[dcn][:, ia * 1024:(ia + 1) * 1024],
                                wd1[:, :], wait=[("pl", pl_pair)])
                    stb = P.dma("gpsimd", wdn_b[dcn][:, ib * 1024:(ib + 1) * 1024],
                                 wd2[:, :])
                    st_a = [sta, stb]
            wdn_state[dcn] = st_a

        def wdn_tail_piece(dcn, k):
            wld = []
            if wdn_state["cast"] is not None:
                wld.append(("ac", wdn_state["cast"]))
            else:
                wld.append(("ac", p0_state["cast"]))
                if p0_state["st"] is not None:
                    wld.append(p0_state["st"])
            if k == 0:
                wld += list(wdn_state[dcn])
            ld = P.dma("sync", wd3[:, :], wdn_b[dcn][:, k * 1024:(k + 1) * 1024],
                       wait=wld)
            cast = P.emit("scalar", lambda sc, dcn=dcn: sc.activation(
                out=wdc[:, :], in_=wd3[:, :], func=AF.Copy,
                scale=c_wsd[:, dcn:dcn + 1]), wait=[ld, ("dv", SU_DV)], inc="ac")
            wdn_state["cast"] = cast
            wpt = [("ac", cast)]
            if wdn_state["ptwar"] is not None:
                wpt.append(("ac", wdn_state["ptwar"]))
            else:
                wpt.append(("pe", A_PE_END))
            last = None
            for jj in range(8):
                last = P.emit("tensor", lambda te, jj=jj: te.transpose(
                    out=pt0[:, jj * 128:(jj + 1) * 128],
                    in_=wdc[:, jj * 128:(jj + 1) * 128], identity=c_id[:, :]),
                    wait=wpt if jj == 0 else (),
                    inc="pe" if jj == 7 else None)
            evl = None
            for jj in range(8):
                jc = k * 8 + jj
                wev = []
                if jj == 0:
                    wev = [("pe", last)]
                    if wdn_state["tail"] is not None:
                        wev.append(wdn_state["tail"])
                evl = P.emit("scalar", lambda sc, jj=jj, jc=jc: sc.activation(
                    out=wds[:, jj * 128:(jj + 1) * 128],
                    in_=pt0[:, jj * 128:(jj + 1) * 128], func=AF.Copy,
                    scale=c_sud[:, jc:jc + 1]),
                    wait=wev, inc="ac" if jj == 7 else None)
            wdn_state["ptwar"] = evl
            st = P.dma(
                "sync",
                ag_wdn_in[k * 8:(k + 1) * 8, :, dcn * 128:(dcn + 1) * 128]
                .rearrange("jc j q -> j jc q"),
                v3(wds, 128), wait=[("ac", evl)])
            wdn_state["tail"] = st
            return evl

        wdn_butterflies(0)

        P.set_phase("ph_C")
        sst = P.dma("gpsimd", ag_s_in.ap().rearrange("one (p q) -> (one p) q", p=128),
                    sfm[:, :], wait=[("dv", B_SCAN_END)])
        CC_S = P.emit("gpsimd", ag(ag_s_in, ag_s), wait=[("dg", sst)], inc="cc")
        sld = P.dma("gpsimd", sall[:, :].rearrange("p (r q) -> p r q", q=RC),
                    ag_s.ap().rearrange("r (p q) -> p r q", p=128), wait=[("cc", CC_S)])
        P.emit("vector", lambda v: v.memset(soff[:, :], 0.0), wait=[("dg", sld)])
        C_OFF = None
        for j in range(NCORES):
            C_OFF = P.emit("vector", lambda v, j=j: v.scalar_tensor_tensor(
                out=soff[:, :], in0=sall[:, j * RC:(j + 1) * RC],
                scalar=c_mask[:, j:j + 1], in1=soff[:, :], op0=OP.mult, op1=OP.add),
                inc="dv" if j == NCORES - 1 else None)
        if "soff" in taps:
            P.dma("sync", dbg["soff"][:, 0:NCORES * RC], sall[:, :], wait=[("dv", C_OFF)])
            P.dma("sync", dbg["soff"][:, NCORES * RC:], soff[:, :], wait=[("dv", C_OFF)])

        CC_WUP = P.emit("gpsimd", ag(ag_wup_in, ag_wup),
                        wait=P.queue_barrier("sync") + P.queue_barrier("gpsimd"), inc="cc")
        CC_WGT = P.emit("gpsimd", ag(ag_wgt_in, ag_wgt),
                        wait=P.queue_barrier("sync") + P.queue_barrier("gpsimd"), inc="cc")

        P.set_phase("ph_wdn2")
        wdn_butterflies(1)

        P.set_phase("ph_D")
        d_cast = {}
        d_gmm = {}
        d_gev = {}
        d_vld = {}
        d_vmm = {}
        d_vgev = {}
        d_go = {}
        for rc in range(RC):
            gb = GB if rc % 2 == 0 else GBb
            d_cast[rc], d_gmm[rc], d_gev[rc] = g_block(
                rc, "D",
                prev_gmm=d_gmm.get(rc - 1) or b_gmm[RC - 1],
                prev_cast=d_cast.get(rc - 1) or b_cast[RC - 1],
                gb_war=d_go.get(rc - 2) or B_SCAN_END,
                gev_war=d_gev.get(rc - 1) or b_gev[RC - 1], gb=gb)
            w = [("cc", CC_WV)]
            if rc >= 2:
                w.append(("pe", d_vmm[rc - 2][3]))
            else:
                w.append(("pe", b_umm[RC - 1][3]))
            d_vld[rc] = P.dma("sync", v3(wb[rc % 2], 128),
                              ag_wv[rc].rearrange("dc p q -> p dc q"), wait=w)
            for lt in range(4):
                wmm = [("dm", d_vld[rc])] if lt == 0 else []
                if rc >= 1:
                    wmm.append(("dv", d_vgev[rc - 1][lt]))
                else:
                    wmm.append(("dv", b_ugev[RC - 1][lt]))
                last = None
                for dc in range(DC):
                    last = P.emit("tensor", lambda te, rc=rc, lt=lt, dc=dc: te.matmul(
                        out=pf[lt][:, :], lhsT=wb[rc % 2][:, dc * 128:(dc + 1) * 128],
                        rhs=xn_tok(dc, lt * 512, 512),
                        start=(dc == 0), stop=(dc == DC - 1)),
                        wait=wmm if dc == 0 else (), inc="pe" if dc == DC - 1 else None)
                d_vmm.setdefault(rc, {})[lt] = last
                d_vgev.setdefault(rc, {})[lt] = P.emit(
                    "vector", lambda v, rc=rc, lt=lt, gb=gb: v.scalar_tensor_tensor(
                        out=VB[:, lt * 512:(lt + 1) * 512], in0=pf[lt][:, :],
                        scalar=c_bv[:, rc:rc + 1], in1=gb[:, lt * 512:(lt + 1) * 512],
                        op0=OP.add, op1=OP.mult),
                    wait=[("pe", d_vmm[rc][lt]), ("ac", d_gev[rc][lt])], inc="dv")
            if rc < 8:
                wdn_tail_piece(0, rc)
            wgo = [("dv", C_OFF)] if rc == 0 else []
            d_go[rc] = P.emit("vector", lambda v, rc=rc: v.scalar_tensor_tensor(
                out=KC[:, rc * T:(rc + 1) * T], in0=KC[:, rc * T:(rc + 1) * T],
                scalar=soff[:, rc:rc + 1], in1=VB[:, :], op0=OP.add, op1=OP.mult),
                wait=wgo, inc="dv")
        D_GO_END = d_go[RC - 1]
        D_PE_END = d_vmm[RC - 1][3]
        if "go" in taps:
            P.dma("sync", dbg["go"], KC[:, :], wait=[("dv", D_GO_END)])

        strm_ctx.close()

        e_ctx = ExitStack()
        wuvb = e_ctx.enter_context(nc.sbuf_tensor("wuvb", [128, 16 * 512], BF16))
        ext = [e_ctx.enter_context(nc.sbuf_tensor(f"ext{i}", [128, 512], F32)) for i in range(2)]
        etb = [e_ctx.enter_context(nc.sbuf_tensor(f"etb{i}", [128, 512], F32)) for i in range(2)]
        E_FENCE = [("pe", D_PE_END), ("dv", D_GO_END), ("ac", P.n("ac"))] \
            + P.queue_barrier("sync")

        P.set_phase("ph_E")
        ttm = HB
        e_uvld = {}
        e_conv = {}
        e_tev = {}
        e_uvmm = {}
        e_easm = {}
        e_eadd = {}
        e_est = {}
        for dt in range(4):
            w = [("cc", CC_WUV)] + (list(E_FENCE) if dt == 0 else [("pe", e_uvmm[(dt - 1, 15)])])
            e_uvld[dt] = P.dma("sync", v3(wuvb, 512),
                               ag_wuv[dt * 16:(dt + 1) * 16].rearrange("b p q -> p b q"),
                               wait=w)
            for q in range(4):
                dc = dt * 4 + q
                base = dc * XST
                xm1 = XN[:, base + 1: base + 1 + T]
                x0 = XN[:, base + 2: base + 2 + T]
                xp1 = XN[:, base + 3: base + 3 + T]
                wcv = []
                if dt == 0 and q == 0:
                    wcv = [("ac", d_gev[RC - 1][3]), ("dv", D_GO_END)]
                P.emit("vector", lambda v, xm1=xm1, dc=dc: v.tensor_scalar(
                    out=GB[:, :], in0=xm1, scalar1=c_cw[:, dc * 3: dc * 3 + 1],
                    scalar2=c_cb[:, dc:dc + 1], op0=OP.mult, op1=OP.add), wait=wcv)
                P.emit("vector", lambda v, x0=x0, dc=dc: v.scalar_tensor_tensor(
                    out=GB[:, :], in0=x0, scalar=c_cw[:, dc * 3 + 1: dc * 3 + 2],
                    in1=GB[:, :], op0=OP.mult, op1=OP.add))
                e_conv[(dt, q)] = P.emit("vector", lambda v, xp1=xp1, dc=dc:
                    v.scalar_tensor_tensor(
                        out=GB[:, :], in0=xp1, scalar=c_cw[:, dc * 3 + 2: dc * 3 + 3],
                        in1=GB[:, :], op0=OP.mult, op1=OP.add), inc="dv")
                for half, pt in [(0, pt0), (1, pt1)]:
                    wtr = [("dv", e_conv[(dt, q)])]
                    if dt == 0 and q == 0:
                        wtr.append(("ac", P.n("ac")))
                    else:
                        prev = e_tev.get((dt, q - 1), e_tev.get((dt - 1, 3)))
                        wtr.append(("ac", prev[half]))
                        if half == 0 and wdn_state["ptwar"] is not None:
                            wtr.append(("ac", wdn_state["ptwar"]))
                    last = None
                    for j in range(8):
                        tl = half * 8 + j
                        last = P.emit("tensor", lambda te, pt=pt, j=j, tl=tl: te.transpose(
                            out=pt[:, j * 128:(j + 1) * 128],
                            in_=GB[:, tl * 128:(tl + 1) * 128], identity=c_id[:, :]),
                            wait=wtr if j == 0 else (), inc="pe" if j == 7 else None)
                    e_conv[(dt, q, half)] = last
                tev = {}
                for half, pt in [(0, pt0), (1, pt1)]:
                    dst = ttm[:, :].rearrange("p (tl q) -> p tl q", q=512)[
                        :, half * 8:(half + 1) * 8, q * 128:(q + 1) * 128]
                    tev[half] = P.emit("scalar", lambda sc, pt=pt, dst=dst: sc.activation(
                        out=dst, in_=pt[:, :].rearrange("p (tl q) -> p tl q", q=128),
                        func=AF.Copy),
                        wait=[("pe", e_conv[(dt, q, half)])], inc="ac")
                e_tev[(dt, q)] = tev
            wdn_tail_piece(1, dt * 2)
            wdn_tail_piece(1, dt * 2 + 1)
            for l in range(NL):
                wmm = []
                if l == 0:
                    wmm = [("dm", e_uvld[dt])]
                if dt == 0 and l < 4:
                    wmm.append(("dv", D_GO_END))
                if (dt, l - 4) in e_easm:
                    wmm.append(("dv", e_easm[(dt, l - 4)]))
                elif dt > 0 and (dt - 1, NL - 4 + l) in e_easm:
                    wmm.append(("dv", e_easm[(dt - 1, NL - 4 + l)]))
                last = None
                for rc in range(RC):
                    last = P.emit("tensor", lambda te, dt=dt, l=l, rc=rc: te.matmul(
                        out=pf[l % 4][:, :],
                        lhsT=KC[:, rc * T + l * 128: rc * T + (l + 1) * 128],
                        rhs=wuvb[:, rc * 512:(rc + 1) * 512],
                        start=(rc == 0), stop=(rc == RC - 1)),
                        wait=wmm if rc == 0 else (), inc="pe" if rc == RC - 1 else None)
                e_uvmm[(dt, l)] = last
                wx = []
                if (dt, l - 2) in e_est:
                    wx.append(e_est[(dt, l - 2)])
                elif dt > 0:
                    wx.append(e_est[(dt - 1, NL - 2 + l)])
                else:
                    wx += list(E_FENCE)
                xld = P.dma("sync", ext[l % 2][:, :],
                            xs[1 + l * 128: 1 + (l + 1) * 128, dt * 512:(dt + 1) * 512],
                            wait=wx)
                wasm = [("pe", e_uvmm[(dt, l)]),
                        ("ac", e_tev[(dt, 3)][0]), ("ac", e_tev[(dt, 3)][1])]
                if (dt, l - 2) in e_eadd:
                    wasm.append(("pl", e_eadd[(dt, l - 2)]))
                elif dt > 0 and (dt - 1, NL - 2 + l) in e_eadd:
                    wasm.append(("pl", e_eadd[(dt - 1, NL - 2 + l)]))
                e_easm[(dt, l)] = P.emit("vector", lambda v, dt=dt, l=l: v.tensor_tensor(
                    out=etb[l % 2][:, :], in0=pf[l % 4][:, :],
                    in1=ttm[:, l * 512:(l + 1) * 512], op=OP.add),
                    wait=wasm, inc="dv")
                wadd = [("dv", e_easm[(dt, l)]), xld]
                e_eadd[(dt, l)] = P.emit("gpsimd", lambda g, dt=dt, l=l: g.tensor_tensor(
                    out=ext[l % 2][:, :], in0=etb[l % 2][:, :], in1=ext[l % 2][:, :],
                    op=OP.add), wait=wadd, inc="pl")
                e_est[(dt, l)] = P.dma("gpsimd",
                                       x1d[l * 128:(l + 1) * 128, dt * 512:(dt + 1) * 512],
                                       ext[l % 2][:, :], wait=[("pl", e_eadd[(dt, l)])])
        CC_WDN = P.emit("gpsimd", ag(ag_wdn_in, ag_wdn),
                        wait=P.queue_barrier("sync") + P.queue_barrier("gpsimd"), inc="cc")
        if "wdn" in taps:
            P.dma("sync", dbg["wdn"], ag_wdn_in.ap(), wait=[("cc", CC_WDN)])
        E_PE_END = e_uvmm[(3, NL - 1)]
        E_DV_END = e_easm[(3, NL - 1)]
        E_ST_END = e_est[(3, NL - 1)]
        if "x1" in taps:
            P.dma("gpsimd", dbg["x1"], x1d.ap(), wait=[("dg", E_ST_END)])

        e_ctx.close()
        wdn_ctx.close()
        p0_ctx.close()
        attn_ctx.close()

        f_ctx = ExitStack()
        XB = f_ctx.enter_context(nc.sbuf_tensor("XB", [128, DC * 1024], BF16))
        ACC = f_ctx.enter_context(nc.sbuf_tensor("ACC", [128, 8 * 2048], F32))
        YR = [f_ctx.enter_context(nc.sbuf_tensor(f"YR{i}", [128, 1024], BF16)) for i in range(8)]
        wupb = [f_ctx.enter_context(nc.sbuf_tensor(f"wupb{i}", [128, DC * 256], BF16)) for i in range(2)]
        wgtb = [f_ctx.enter_context(nc.sbuf_tensor(f"wgtb{i}", [128, DC * 256], BF16)) for i in range(2)]
        wdng = [f_ctx.enter_context(nc.sbuf_tensor(f"wdng{i}", [128, 8 * 512], BF16)) for i in range(2)]
        gsb = f_ctx.enter_context(nc.sbuf_tensor("gsb", [128, 1024], BF16))
        xft = [f_ctx.enter_context(nc.sbuf_tensor(f"xft{i}", [128, D], F32)) for i in range(2)]
        VB2 = f_ctx.enter_context(nc.sbuf_tensor("VB2", [128, T], BF16))
        opb = [f_ctx.enter_context(nc.sbuf_tensor(f"opb{i}", [128, 512], F32)) for i in range(2)]
        F_FENCE = [("pe", E_PE_END), ("dv", E_DV_END), ("ac", P.n("ac")),
                   ("pl", P.n("pl"))] \
            + P.queue_barrier("gpsimd") + P.queue_barrier("sync")

        vbs = [VB, VB2]
        f_xld = {}
        f_sq = {}
        f_sqr = {}
        f_ap = {}
        f_tr = {}
        f_ev = {}
        g_upmm = {}
        g_gtmm = {}
        g_gsev = {}
        g_ystt = {}
        g_dnmm = {}
        g_flush = {}
        g_wld = {}
        g_dnld = {}
        g_ost = {}
        g_oasm = {}
        XB_READY = {}

        for hf in range(2):
            P.set_phase(f"ph_F{hf}")
            def f_finish(t, hf=hf):
                vb = vbs[t % 2]
                w_ap = []
                if (hf, t - 2) in f_tr:
                    w_ap = [("pe", f_tr[(hf, t - 2)][1])]
                elif hf == 1:
                    w_ap = [("pe", f_tr[(0, 6 + t)][1])]
                f_ap[(hf, t)] = ln_finish(xft[t % 2], t, vb, 128, f_sqr[(hf, t)], w_ap)
                for half, pt in [(0, pt0), (1, pt1)]:
                    wtr = [("dv", f_ap[(hf, t)])]
                    if (hf, t - 1) in f_ev:
                        wtr.append(("ac", f_ev[(hf, t - 1)][half]))
                    elif hf == 1:
                        wtr.append(("ac", f_ev[(0, 7)][half]))
                    elif hf == 0 and t == 0:
                        wtr.append(("ac", P.n("ac")))
                    last = None
                    for j in range(8):
                        dc = half * 8 + j
                        last = P.emit("tensor", lambda te, pt=pt, j=j, dc=dc, vb=vb:
                                      te.transpose(out=pt[:, j * 128:(j + 1) * 128],
                                                   in_=vb[:, dc * 128:(dc + 1) * 128],
                                                   identity=c_id[:, :]),
                                      wait=wtr if j == 0 else (),
                                      inc="pe" if j == 7 else None)
                    f_tr.setdefault((hf, t), {})[half] = last
                ev = {}
                for half, pt in [(0, pt0), (1, pt1)]:
                    dst = XB[:, :].rearrange("p (dc q) -> p dc q", q=1024)[
                        :, half * 8:(half + 1) * 8, t * 128:(t + 1) * 128]
                    ev[half] = P.emit("scalar", lambda sc, pt=pt, dst=dst: sc.activation(
                        out=dst, in_=pt[:, :].rearrange("p (dc q) -> p dc q", q=128),
                        func=AF.Copy), wait=[("pe", f_tr[(hf, t)][half])], inc="ac")
                f_ev[(hf, t)] = ev

            for t in range(8):
                w = list(F_FENCE) if (hf == 0 and t < 2) else []
                if (hf, t - 2) in f_ap:
                    w += [("dv", f_ap[(hf, t - 2)]), ("ac", f_sq[(hf, t - 2)])]
                elif hf == 1 and t < 2:
                    w += [("dv", g_oasm[(0, 7, 3)])]
                    w += [("dv", f_ap[(0, 6 + t)]), ("ac", f_sq[(0, 6 + t)])]
                f_xld[(hf, t)] = P.dma("gpsimd", xft[t % 2][:, :],
                                       x1d[hf * 1024 + t * 128: hf * 1024 + (t + 1) * 128, :],
                                       wait=w)
                f_sq[(hf, t)], f_sqr[(hf, t)] = ln_stats(
                    xft[t % 2], t, 128, [f_xld[(hf, t)]], [f_xld[(hf, t)]])
                if t >= 1:
                    f_finish(t - 1)
            f_finish(7)
            hevs = {}
            for dc in range(DC):
                for lt2 in range(2):
                    k = dc * 2 + lt2
                    bank = pf[k % 2]
                    if k < 2:
                        w = [("ac", f_ev[(hf, 7)][0]), ("ac", f_ev[(hf, 7)][1]),
                             ("dv", CONST_DV)]
                    else:
                        w = [("ac", hevs[k - 2])]
                    mm = P.emit("tensor", lambda te, dc=dc, lt2=lt2, bank=bank: te.matmul(
                        out=bank[:, :], lhsT=c_hsu[:, dc * 128:(dc + 1) * 128],
                        rhs=XB[:, dc * 1024 + lt2 * 512: dc * 1024 + (lt2 + 1) * 512],
                        start=True, stop=True), wait=w, inc="pe")
                    hevs[k] = P.emit("scalar", lambda sc, dc=dc, lt2=lt2, bank=bank:
                                     sc.activation(
                                         out=XB[:, dc * 1024 + lt2 * 512: dc * 1024 + (lt2 + 1) * 512],
                                         in_=bank[:, :], func=AF.Copy),
                                     wait=[("pe", mm)], inc="ac")
            bstage = None
            for hp in [1, 2, 4, 8]:
                nb = DC // (2 * hp)
                vv = XB[:, :].rearrange("p (nb two hq) -> p nb two hq", two=2, hq=hp * 1024)
                a_, b_ = vv[:, :, 0, :], vv[:, :, 1, :]
                w = [("ac", hevs[31])] if hp == 1 else []
                P.emit("vector", lambda v, a_=a_, b_=b_: v.tensor_tensor(
                    out=a_, in0=a_, in1=b_, op=OP.add), wait=w)
                bstage = P.emit("vector", lambda v, a_=a_, b_=b_: v.scalar_tensor_tensor(
                    out=b_, in0=b_, scalar=-2.0, in1=a_, op0=OP.mult, op1=OP.add),
                    inc="dv")
            XB_READY[hf] = bstage
            if "xb" in taps and hf == 0:
                P.dma("sync", dbg["xb"], XB[:, :], wait=[("dv", bstage)])

            P.set_phase(f"ph_G{hf}")
            for ic in range(ICN):
                pr, sl = ic // 2, ic % 2
                if sl == 0:
                    def war(mmdict):
                        key = (hf, 2 * pr - 3)
                        if key in mmdict:
                            return [("pe", mmdict[key])]
                        if hf == 1:
                            return [("pe", mmdict[(0, ICN - 1)])]
                        return []
                    w = [("cc", CC_WUP)] + war(g_upmm)
                    if hf == 0 and pr == 0:
                        w += list(F_FENCE)
                    g_wld[(hf, pr, "u")] = P.dma(
                        "sync", v3(wupb[pr % 2], 256),
                        ag_wup[pr].rearrange("dc p q -> p dc q"), wait=w)
                    w2 = [("cc", CC_WGT)] + war(g_gtmm)
                    if hf == 0 and pr == 0:
                        w2 += list(F_FENCE)
                    g_wld[(hf, pr, "g")] = P.dma(
                        "sync", wgtb[pr % 2][:, :],
                        ag_wgt[pr].rearrange("dc p q -> p dc q"), wait=w2)
                for lt2 in range(2):
                    wmm = []
                    if lt2 == 0:
                        wmm = [("dm", g_wld[(hf, pr, "u")]), ("dv", XB_READY[hf])]
                    if (hf, ic - 1, lt2) in g_ystt:
                        wmm.append(("dv", g_ystt[(hf, ic - 1, lt2)]))
                    elif hf == 1:
                        wmm.append(("dv", g_ystt[(0, ICN - 1, lt2)]))
                    last = None
                    for dc in range(DC):
                        last = P.emit("tensor", lambda te, pr=pr, sl=sl, lt2=lt2, dc=dc:
                                      te.matmul(
                                          out=pf[lt2][:, :],
                                          lhsT=wupb[pr % 2][:, dc * 256 + sl * 128: dc * 256 + (sl + 1) * 128],
                                          rhs=XB[:, dc * 1024 + lt2 * 512: dc * 1024 + (lt2 + 1) * 512],
                                          start=(dc == 0), stop=(dc == DC - 1)),
                                      wait=wmm if dc == 0 else (),
                                      inc="pe" if dc == DC - 1 else None)
                g_upmm[(hf, ic)] = last
                for lt2 in range(2):
                    wmm = []
                    if lt2 == 0:
                        wmm = [("dm", g_wld[(hf, pr, "g")])]
                    if (hf, ic - 1, lt2) in g_gsev:
                        wmm.append(("ac", g_gsev[(hf, ic - 1, lt2)]))
                    elif hf == 1:
                        wmm.append(("ac", g_gsev[(0, ICN - 1, lt2)]))
                    last = None
                    for dc in range(DC):
                        last = P.emit("tensor", lambda te, pr=pr, sl=sl, lt2=lt2, dc=dc:
                                      te.matmul(
                                          out=pf[2 + lt2][:, :],
                                          lhsT=wgtb[pr % 2][:, dc * 256 + sl * 128: dc * 256 + (sl + 1) * 128],
                                          rhs=XB[:, dc * 1024 + lt2 * 512: dc * 1024 + (lt2 + 1) * 512],
                                          start=(dc == 0), stop=(dc == DC - 1)),
                                      wait=wmm if dc == 0 else (),
                                      inc="pe" if dc == DC - 1 else None)
                    g_gtmm[(hf, ic)] = last
                    wev = [("pe", last)]
                    if (hf, ic - 1, lt2) in g_ystt:
                        wev.append(("dv", g_ystt[(hf, ic - 1, lt2)]))
                    elif hf == 1:
                        wev.append(("dv", g_ystt[(0, ICN - 1, lt2)]))
                    g_gsev[(hf, ic, lt2)] = P.emit(
                        "scalar", lambda sc, lt2=lt2, ic=ic: sc.activation(
                            out=gsb[:, lt2 * 512:(lt2 + 1) * 512], in_=pf[2 + lt2][:, :],
                            func=AF.Silu, scale=c_wsg[:, ic:ic + 1]),
                        wait=wev, inc="ac")
                for lt2 in range(2):
                    wst = [("pe", g_upmm[(hf, ic)]), ("ac", g_gsev[(hf, ic, lt2)])]
                    gk = ic - 8
                    if gk >= 0:
                        wst.append(("pe", g_dnmm[(hf, gk // 8, 3, 7)]))
                    elif hf == 1:
                        wst.append(("pe", g_dnmm[(0, 7, 3, 7)]))
                    g_ystt[(hf, ic, lt2)] = P.emit(
                        "vector", lambda v, ic=ic, lt2=lt2: v.scalar_tensor_tensor(
                            out=YR[ic % 8][:, lt2 * 512:(lt2 + 1) * 512],
                            in0=pf[lt2][:, :], scalar=c_wsu[:, ic:ic + 1],
                            in1=gsb[:, lt2 * 512:(lt2 + 1) * 512],
                            op0=OP.mult, op1=OP.mult),
                        wait=wst, inc="dv")
                if "y0" in taps and hf == 0 and ic == 0:
                    P.dma("sync", dbg["y0"], YR[0][:, :],
                          wait=[("dv", g_ystt[(0, 0, 1)])])
                    pass
                if ic % 8 == 7:
                    gidx = ic // 8
                    for dtile in range(4):
                        w = [("cc", CC_WDN)]
                        if hf == 0 and gidx == 0 and dtile < 2:
                            w += list(F_FENCE)
                        if dtile >= 2:
                            w.append(("pe", g_dnmm[(hf, gidx, dtile - 2, 7)]))
                        elif gidx >= 1:
                            w.append(("pe", g_dnmm[(hf, gidx - 1, dtile + 2, 7)]))
                        elif hf == 1:
                            w.append(("pe", g_dnmm[(0, 7, dtile + 2, 7)]))
                        toks = []
                        for r in range(2):
                            src = ag_wdn.ap()[2 * dtile + r,
                                              gidx * 8:(gidx + 1) * 8, :, :] \
                                .rearrange("ic p q -> p ic q")
                            dst = wdng[dtile % 2][:, :].rearrange(
                                "p (ic q) -> p ic q", q=512)[:, :, r * 256:(r + 1) * 256]
                            toks.append(P.dma(
                                "gpsimd", dst, src, wait=w if r == 0 else ()))
                        g_dnld[(hf, gidx, dtile)] = toks
                        for lsub in range(8):
                            bank = pf[4 + (dtile * 8 + lsub) % 2]
                            wmm = list(g_dnld[(hf, gidx, dtile)]) + [
                                   ("dv", g_ystt[(hf, ic, 1)])]
                            fkey = (hf, gidx, dtile, lsub - 2)
                            fkey2 = (hf, gidx, dtile - 1, lsub + 6)
                            fkey3 = (hf, gidx - 1, 3, lsub + 6) if gidx >= 1 else \
                                ((0, 7, 3, lsub + 6) if hf == 1 else None)
                            if fkey in g_flush:
                                wmm.append(("dv", g_flush[fkey]))
                            elif fkey2 in g_flush:
                                wmm.append(("dv", g_flush[fkey2]))
                            elif fkey3 is not None and fkey3 in g_flush:
                                wmm.append(("dv", g_flush[fkey3]))
                            last = None
                            for k in range(8):
                                last = P.emit("tensor",
                                              lambda te, k=k, lsub=lsub, dtile=dtile, bank=bank:
                                              te.matmul(
                                                  out=bank[:, :],
                                                  lhsT=YR[k][:, lsub * 128:(lsub + 1) * 128],
                                                  rhs=wdng[dtile % 2][:, k * 512:(k + 1) * 512],
                                                  start=(k == 0), stop=(k == 7)),
                                              wait=wmm if k == 0 else (),
                                              inc="pe" if k == 7 else None)
                            g_dnmm[(hf, gidx, dtile, lsub)] = last
                            accsl = ACC[:, lsub * 2048 + dtile * 512:
                                        lsub * 2048 + (dtile + 1) * 512]
                            if gidx == 0:
                                g_flush[(hf, gidx, dtile, lsub)] = P.emit(
                                    "vector", lambda v, accsl=accsl, bank=bank:
                                    v.tensor_copy(out=accsl, in_=bank[:, :]),
                                    wait=[("pe", last)], inc="dv")
                            else:
                                g_flush[(hf, gidx, dtile, lsub)] = P.emit(
                                    "vector", lambda v, accsl=accsl, bank=bank:
                                    v.tensor_tensor(out=accsl, in0=accsl, in1=bank[:, :],
                                                    op=OP.add),
                                    wait=[("pe", last)], inc="dv")
            if "acc" in taps and hf == 0:
                P.dma("sync", dbg["acc"], ACC[:, :],
                      wait=[("dv", g_flush[(0, 7, 3, 7)])])
            P.set_phase(f"ph_out{hf}")
            for lsub in range(8):
                w = [("dv", g_flush[(hf, 7, 3, lsub)])]
                if lsub >= 2:
                    w.append(("dv", g_oasm[(hf, lsub - 2, 3)]))
                xl = P.dma("gpsimd", xft[lsub % 2][:, :],
                           x1d[hf * 1024 + lsub * 128: hf * 1024 + (lsub + 1) * 128, :],
                           wait=w + ([("ac", f_sq[(hf, 7)]), ("dv", f_ap[(hf, 7)])]
                                     if lsub < 2 else []))
                for piece in range(4):
                    w2 = [("dg", xl), ("dv", g_flush[(hf, 7, piece, lsub)])]
                    if (hf, lsub - 1, piece) in g_ost or lsub * 4 + piece >= 2:
                        prevk = (hf, lsub, piece - 2) if piece >= 2 else (hf, lsub - 1, piece + 2)
                        if prevk in g_ost:
                            w2.append(("dg", g_ost[prevk]))
                    g_oasm[(hf, lsub, piece)] = P.emit(
                        "vector", lambda v, lsub=lsub, piece=piece: v.tensor_tensor(
                            out=opb[piece % 2][:, :],
                            in0=ACC[:, lsub * 2048 + piece * 512: lsub * 2048 + (piece + 1) * 512],
                            in1=xft[lsub % 2][:, piece * 512:(piece + 1) * 512],
                            op=OP.add), wait=w2, inc="dv")
                    g_ost[(hf, lsub, piece)] = P.dma(
                        "gpsimd",
                        out[hf * 1024 + lsub * 128: hf * 1024 + (lsub + 1) * 128,
                            piece * 512:(piece + 1) * 512],
                        opb[piece % 2][:, :], wait=[("dv", g_oasm[(hf, lsub, piece)])])

        P.emit("gpsimd", lambda g: g.tensor_copy(out=sprt[:, 1:2], in_=sprt[:, 0:1]),
               wait=P.queue_barrier("gpsimd") + P.queue_barrier("sync") +
               [("cc", P.n("cc")), ("dv", P.n("dv")), ("ac", P.n("ac")),
                ("pe", P.n("pe"))],
               inc="pl")
        f_ctx.close()

        P.check()
        block = e(nc.Block())
        P.finalize(block)

    return nc



def _fm(vec, width):
    return np.ascontiguousarray(vec.reshape(width, 128).T)


def _prep_core(c, x2d, W):
    t0 = c * T
    rows = np.zeros((T + 2, D), np.float32)
    rows[1:T + 1] = x2d[t0:t0 + T]
    if c > 0:
        rows[0] = x2d[t0 - 1]
    if c < NCORES - 1:
        rows[T + 1] = x2d[t0 + T]

    wuT = W["wu"].T
    wu_u_t = np.ascontiguousarray(
        wuT.reshape(DC, 128, RC, 128).transpose(2, 0, 1, 3))
    wuv_t = np.ascontiguousarray(
        wuT.reshape(RC, 128, 4, 512).transpose(2, 0, 1, 3)).reshape(64, 128, 512)
    wvT = W["wv"].T
    wv_t = np.ascontiguousarray(wvT.reshape(DC, 128, RC, 128).transpose(2, 0, 1, 3))
    wg1_t = np.ascontiguousarray(
        W["wg1"].T.reshape(DC, 128, HCN, 128).transpose(2, 0, 1, 3))
    wg2_t = np.ascontiguousarray(
        W["wg2"].T.reshape(HCN, 128, RC, 128).transpose(2, 0, 1, 3))
    wupT = W["wup"].T
    wup_t = np.ascontiguousarray(
        wupT.reshape(DC, 128, 32, 256).transpose(2, 0, 1, 3))
    wgtT = W["wgt"].T
    wgt_t = np.ascontiguousarray(
        wgtT.reshape(DC, 128, 32, 256).transpose(2, 0, 1, 3))
    wdn_t = np.ascontiguousarray(
        W["wdn"][c * 256:(c + 1) * 256].reshape(2, 128, I_DIM))

    mask = np.zeros((128, NCORES), np.float32)
    mask[:, :c] = 1.0

    return {
        "xs": rows,
        "wu_u": wu_u_t,
        "wg1": wg1_t,
        "wg2": wg2_t,
        "wv_sl": np.ascontiguousarray(wv_t[2 * c: 2 * c + 2]),
        "wuv_sl": np.ascontiguousarray(wuv_t[8 * c: 8 * (c + 1)]),
        "wup_sl": np.ascontiguousarray(wup_t[4 * c: 4 * (c + 1)]),
        "wgt_sl": np.ascontiguousarray(wgt_t[4 * c: 4 * (c + 1)]),
        "wdn_sl": wdn_t,
        "bu": _fm(W["bu"], RC), "bv": _fm(W["bv"], RC),
        "bg1": _fm(W["bg1"], HCN), "bg2": _fm(W["bg2"], RC),
        "cw": np.ascontiguousarray(
            W["cw"].reshape(DC, 128, 3).transpose(1, 0, 2).reshape(128, DC * 3)),
        "cb": _fm(W["cb"], DC),
        "su": _fm(W["su"], DC),
        "sud": _fm(W["sud"], ICN),
        "wsu": _fm(W["wsu"], ICN),
        "wsg": _fm(W["wsg"], ICN),
        "wsd": np.ascontiguousarray(_fm(W["wsd"], DC)[:, 2 * c: 2 * c + 2]),
        "maskt": mask,
        "ident": np.eye(128, dtype=ml_dtypes.bfloat16),
        "h128": W["h128"],
    }


def _hadamard128():
    h = np.array([[1.0]], np.float32)
    while h.shape[0] < 128:
        h = np.block([[h, h], [h, -h]])
    return h.astype(ml_dtypes.bfloat16)


def kernel(L0_attn_x, L0_attn_Wu, L0_attn_bu, L0_attn_Wv, L0_attn_bv,
           L0_attn_Wg1, L0_attn_bg1, L0_attn_Wg2, L0_attn_bg2,
           L0_attn_conv_w, L0_attn_conv_b,
           L0_ffn_SU_ug, L0_ffn_SU_dp, L0_ffn_Wscale_ug, L0_ffn_Wscale_dp,
           L0_ffn_W_up, L0_ffn_W_gate, L0_ffn_W_down):
    x2d = np.ascontiguousarray(np.asarray(L0_attn_x, np.float32).reshape(NCORES * T, D))
    W = {
        "wu": np.asarray(L0_attn_Wu, np.float32),
        "wv": np.asarray(L0_attn_Wv, np.float32),
        "wg1": np.asarray(L0_attn_Wg1, np.float32),
        "wg2": np.asarray(L0_attn_Wg2, np.float32),
        "wup": np.asarray(L0_ffn_W_up, np.float32),
        "wgt": np.asarray(L0_ffn_W_gate, np.float32),
        "wdn": np.asarray(L0_ffn_W_down, np.float32),
        "bu": np.asarray(L0_attn_bu, np.float32),
        "bv": np.asarray(L0_attn_bv, np.float32),
        "bg1": np.asarray(L0_attn_bg1, np.float32),
        "bg2": np.asarray(L0_attn_bg2, np.float32),
        "cw": np.asarray(L0_attn_conv_w, np.float32),
        "cb": np.asarray(L0_attn_conv_b, np.float32),
        "su": np.asarray(L0_ffn_SU_ug, np.float32),
        "sud": np.asarray(L0_ffn_SU_dp, np.float32),
        "wsu": np.asarray(L0_ffn_Wscale_ug, np.float32)[:I_DIM],
        "wsg": np.asarray(L0_ffn_Wscale_ug, np.float32)[I_DIM:],
        "wsd": np.asarray(L0_ffn_Wscale_dp, np.float32),
        "h128": _hadamard128(),
    }
    in_maps = [_prep_core(c, x2d, W) for c in range(NCORES)]
    nc = build_program()
    res = run_bass_kernel_spmd(nc, in_maps, core_ids=list(range(NCORES)))
    outp = np.concatenate([res.results[c]["out"] for c in range(NCORES)], axis=0)
    if DEBUG:
        kernel.last_results = res.results
    return outp.reshape(1, NCORES * T, D)
